# version 01
# baseline (speedup 1.0000x reference)
import contextlib
import numpy as np
import concourse.bass as bass
import concourse.mybir as mybir
from concourse.bass_utils import run_bass_kernel_spmd

F32 = mybir.dt.float32
BF16 = mybir.dt.bfloat16
AF = mybir.ActivationFunctionType
ALU = mybir.AluOpType
AX = mybir.AxisListType.X

P = 128
T = 2048
D = 1024
DI = 2048
NST = 16
DTR = 64
TC = 256
NCH = T // TC
NT = T // P
KD = D // P
KDI = DI // P
NEXP = 8
HMOE = 2048
TB = 256
NTB = T // TB
HEADS = 16
HD = 64
ACH = 128
NACH = T // ACH
LEV = 3
HID = 4096
EPS = 1e-5
BIG = 1e30


class DmaStream:

    def __init__(self, prog, name, depth):
        self.p = prog
        self.name = name
        self.d = depth
        self.sems = [prog.ctx.enter_context(prog.nc.semaphore(f"dma_{name}{i}"))
                     for i in range(depth)]
        self.n = 0

    def issue(self, eng, out, in_, **kw):
        i = self.n
        slot = i % self.d
        if i >= self.d:
            eng.wait_ge(self.sems[slot], 16 * (i // self.d))
        eng.dma_start(out=out, in_=in_, **kw).then_inc(self.sems[slot], 16)
        self.n += 1
        return i

    def wait(self, eng, i=None):
        if i is None:
            i = self.n - 1
        if i < 0:
            return
        eng.wait_ge(self.sems[i % self.d], 16 * (i // self.d + 1))

    def wait_all(self, eng):
        for s in range(min(self.d, self.n)):
            last = (self.n - 1 - s)
            eng.wait_ge(self.sems[last % self.d], 16 * (last // self.d + 1))


class DmaGroup:

    def __init__(self, prog, name):
        self.sem = prog.ctx.enter_context(prog.nc.semaphore(f"dmg_{name}"))
        self.n = 0
        self.closed = False

    def issue(self, eng, out, in_, **kw):
        assert not self.closed
        eng.dma_start(out=out, in_=in_, **kw).then_inc(self.sem, 16)
        self.n += 1

    def wait_all(self, eng):
        self.closed = True
        eng.wait_ge(self.sem, 16 * self.n)


class Prog:
    ENGNAME = {"t": "tensor", "v": "vector", "s": "scalar", "g": "gpsimd", "q": "sync"}

    def __init__(self, dbg_spec=()):
        self.nc = bass.Bass("TRN2", target_bir_lowering=False,
                            detect_race_conditions=False)
        self.ctx = contextlib.ExitStack()
        nc = self.nc
        self.sem = {}
        for e in ("t", "v", "s", "g", "q"):
            self.sem[e] = self.ctx.enter_context(nc.semaphore("sem_" + e))
        self.cnt = {k: 0 for k in self.sem}
        self.ev = {}
        self.streams = {}
        self.phase = None
        self.dbg = {}
        for name, shape, dt in dbg_spec:
            self.dbg[name] = nc.dram_tensor("dbg_" + name, list(shape), dt,
                                            kind="ExternalOutput").ap()

    def stream(self, name, depth=2):
        if name not in self.streams:
            self.streams[name] = DmaStream(self, name, depth)
        return self.streams[name]

    def group(self, name):
        if name not in self.streams:
            self.streams[name] = DmaGroup(self, name)
        return self.streams[name]

    def mark(self, name, instr, sem, inc=1):
        instr.then_inc(self.sem[sem], inc)
        self.cnt[sem] += inc
        self.ev.setdefault(name, []).append((sem, self.cnt[sem]))

    def wait(self, eng, name, idx=-1):
        sname, val = self.ev[name][idx]
        eng.wait_ge(self.sem[sname], val)

    def waitn(self, eng, name):
        if name in self.ev:
            self.wait(eng, name, -1)

    def sec(self, e):
        return getattr(self.block, self.ENGNAME[e])

    def barrier(self):
        snap = {}
        for e in ("t", "v", "s", "g", "q"):
            @self.sec(e)
            def _(eng, e=e):
                if e == "q":
                    eng.sem_inc(self.sem[e], 1)
                else:
                    eng.drain().then_inc(self.sem[e], 1)
            self.cnt[e] += 1
            snap[e] = self.cnt[e]
        for e in ("t", "v", "s", "g", "q"):
            @self.sec(e)
            def _(eng):
                for i2, e2 in enumerate(("t", "v", "s", "g", "q")):
                    if i2:
                        eng.nop()
                    eng.wait_ge(self.sem[e2], snap[e2])

    def sb(self, name, shape, dt, persistent=False):
        tgt = self.ctx if (persistent or self.phase is None) else self.phase
        return tgt.enter_context(self.nc.sbuf_tensor(name, shape, dt))

    def open_phase(self):
        assert self.phase is None
        self.phase = contextlib.ExitStack()

    def close_phase(self):
        self.phase.close()
        self.phase = None

    def pst(self, name, shape):
        return self.ctx.enter_context(self.nc.psum_tensor(name, shape, F32))

    def din(self, name, shape, dt=F32):
        return self.nc.dram_tensor(name, list(shape), dt, kind="ExternalInput").ap()

    def scrh(self, name, shape, dt):
        return self.nc.dram_tensor(name, list(shape), dt, kind="Internal").ap()

    def ln_stats_apply(self, i, xt, nt, st, tag):
        p = self

        @p.sec("v")
        def _(v):
            v.reduce_sum(out=st[:, 0:1], in_=xt, axis=AX)
            v.tensor_mul(nt, xt, xt)
            v.reduce_sum(out=st[:, 1:2], in_=nt, axis=AX)
            v.drain()
            v.tensor_scalar(out=st[:, 0:2], in0=st[:, 0:2], scalar1=1.0 / D,
                            scalar2=None, op0=ALU.mult)
            v.drain()
            v.tensor_mul(st[:, 2:3], st[:, 0:1], st[:, 0:1])
            v.drain()
            v.tensor_sub(st[:, 1:2], st[:, 1:2], st[:, 2:3])
            v.drain()
            p.mark(f"{tag}_var{i}", v.tensor_scalar(
                out=st[:, 1:2], in0=st[:, 1:2], scalar1=1.0, scalar2=EPS,
                op0=ALU.mult, op1=ALU.add), "v")

        @p.sec("s")
        def _(s):
            p.wait(s, f"{tag}_var{i}")
            s.activation(out=st[:, 5:6], in_=st[:, 1:2], func=AF.Ln)
            p.mark(f"{tag}_rsq{i}", s.activation(
                out=st[:, 4:5], in_=st[:, 5:6], func=AF.Exp, scale=-0.5), "s")

        @p.sec("v")
        def _(v):
            p.wait(v, f"{tag}_rsq{i}")
            p.mark(f"{tag}_xn{i}", v.scalar_tensor_tensor(
                out=nt, in0=xt, scalar=st[:, 0:1],
                in1=st[:, 4:5].broadcast_to([P, D]),
                op0=ALU.subtract, op1=ALU.mult), "v")


def build(dbg_spec=(), upto="p1"):
    p = Prog(dbg_spec)
    nc = p.nc

    p.x0_h = p.din("x0", (T, D))
    p.w_inT_h = p.din("w_inT", (D, 2 * DI))
    p.conv_w_h = p.din("conv_w", (2, P, KDI * 4))
    p.conv_b_h = p.din("conv_b", (2, P, KDI))
    p.wxT_h = p.din("wxT", (2, DI, 112))
    p.wdtT_h = p.din("wdtT", (2, DTR, DI))
    p.bdt_h = p.din("bdt", (2, P, KDI))
    p.a_log_h = p.din("a_log", (2, P, KDI * NST))
    p.d_skip_h = p.din("d_skip", (2, P, KDI))
    p.w_outT_h = p.din("w_outT", (DI, D))
    p.wr_T_h = p.din("wr_T", (D, NEXP))
    p.br_h = p.din("br", (1, NEXP))
    p.biases_h = p.din("biases_n", (1, NEXP))
    p.wl1_h = p.din("wl1", (NEXP, D, 2 * HMOE))
    p.bl1_h = p.din("bl1", (NEXP, P, 32))
    p.wl2_h = p.din("wl2", (NEXP, HMOE, D))
    p.bl2_h = p.din("bl2", (NEXP, P, 8))
    p.bl2r_h = p.din("bl2r", (NEXP, D))
    p.ln1_g_h = p.din("ln1_g", (P, KD))
    p.ln1_b_h = p.din("ln1_b", (P, KD))
    p.wq_h = p.din("wq", (D, D))
    p.wk_h = p.din("wk", (D, D))
    p.wv_h = p.din("wv", (D, D))
    p.wo_h = p.din("wo_kt", (D, D))
    p.ln2_g_h = p.din("ln2_g", (P, KD))
    p.ln2_b_h = p.din("ln2_b", (P, KD))
    p.cw1_h = p.din("cms_w1", (LEV, D, HID))
    p.cb1_h = p.din("cms_b1", (LEV, P, HID // P))
    p.cw2_h = p.din("cms_w2", (LEV, HID, D))
    p.cb2_h = p.din("cms_b2", (LEV, P, KD))
    p.fw_h = p.din("ffn_wT", (D, D))
    p.fb_h = p.din("ffn_b", (P, KD))
    p.fvr_h = p.din("ffn_vr", (D, 32))
    p.fcT_h = p.din("ffn_cT", (32, D))
    p.out_h = nc.dram_tensor("out", [T, D], F32, kind="ExternalOutput").ap()

    p.xz_h = p.scrh("xz_s", (2 * DI, T + 8), BF16)
    p.xzr_h = p.scrh("xzr_s", (2 * DI, T + 8), BF16)
    p.outf_h = p.scrh("outf_s", (DI, T), BF16)
    p.outb_h = p.scrh("outb_s", (DI, T), BF16)
    p.ymoe_h = p.scrh("ymoe_s", (P, NEXP * T), F32)
    p.x1_h = p.scrh("x1_s", (T, D), F32)
    p.xn1fm_h = p.scrh("xn1_fm_s", (D, T), BF16)
    p.xn1tm_h = p.scrh("xn1_tm_s", (T, D), F32)
    p.g1_h = p.scrh("g1_s", (P, KD * T), BF16)
    p.x1i_h = p.scrh("x1i_s", (P, KD * T), F32)
    p.ln2fm_h = p.scrh("ln2_fm_s", (D, T), BF16)
    p.val_h = p.scrh("val_s", (P, P), F32)
    p.sb_h = p.scrh("sb_s", (P, KD * T), F32)

    p.ident_c = nc.inline_tensor(np.eye(P, dtype=np.float32), "ident").ap()
    p.ones_c = nc.inline_tensor(np.ones((1, P), dtype=np.float32), "ones_row").ap()
    import ml_dtypes as _mld
    p.ones16_c = nc.inline_tensor(np.ones((1, P), dtype=_mld.bfloat16), "ones16_row").ap()
    p.triuT_c = nc.inline_tensor(
        np.triu(np.ones((ACH, ACH), dtype=np.float32)), "triuT").ap()
    oh = np.zeros((NST, NST * P), np.float32)
    for n in range(NST):
        oh[n, n * P:(n + 1) * P] = 1.0
    p.onehot_c = nc.inline_tensor(oh, "onehot16").ap()
    p.onehotb_c = nc.inline_tensor(oh.astype(_mld.bfloat16), "onehot16b").ap()
    p.onesc16_c = nc.inline_tensor(np.ones((P, 1), dtype=_mld.bfloat16), "onesc16").ap()
    p.identb_c = nc.inline_tensor(np.eye(P, dtype=_mld.bfloat16), "identb").ap()

    p.val_fm = p.sb("val_fm", [P, P], F32, persistent=True)
    p.ident_sb = p.sb("ident_sb", [P, P], F32, persistent=True)
    p.ones_sb = p.sb("ones_sb", [1, P], F32, persistent=True)
    p.ones16_sb = p.sb("ones16_sb", [1, P], BF16, persistent=True)
    p.onehotb_sb = p.sb("onehotb_sb", [NST, NST * P], BF16, persistent=True)
    p.triuT_sb = p.sb("triuT_sb", [ACH, ACH], F32, persistent=True)
    p.identb_sb = p.sb("identb_sb", [P, P], BF16, persistent=True)
    p.xn_scope = contextlib.ExitStack()
    p.xn_fm = p.xn_scope.enter_context(nc.sbuf_tensor("xn_fm", [P, KD * T], BF16))

    p.ps_a = p.pst("ps_a", [P, 512])
    p.ps_b = p.pst("ps_b", [P, 512])
    p.ps_t = p.pst("ps_t", [P, 1024])
    p.ps_y = p.pst("ps_y", [P, 2048])

    blk = nc.Block()
    p.block = blk.__enter__()

    p.open_phase()
    emit_p0(p)
    p.barrier()
    p.close_phase()
    if upto != "p0nr":
        p.open_phase()
        emit_router(p, stage=float(upto[1:]) if upto.startswith("r") else 9)
        p.barrier()
        p.close_phase()
    if upto not in ("p0", "p0nr") and not upto.startswith("r"):
        p.open_phase()
        emit_p1(p)
        p.barrier()
        p.close_phase()
    if upto in ("p2a", "p2", "p3", "l1a", "l1b", "l1c", "full"):
        p.open_phase()
        emit_p2a_moe(p)
        p.barrier()
        p.close_phase()
        if upto in ("p2", "p3", "l1a", "l1b", "l1c", "full"):
            p.xn_scope.close()
            p.open_phase()
            emit_p2b(p)
            p.close_phase()
        if upto in ("l1a", "l1b", "l1c", "full"):
            p.l1_scope = contextlib.ExitStack()
            p.xn1_fm_sb = p.l1_scope.enter_context(
                p.nc.sbuf_tensor("xn1_fm", [P, KD * T], BF16))
            p.ln1_fm_sb = p.l1_scope.enter_context(
                p.nc.sbuf_tensor("ln1_fm", [P, KD * T], BF16))
            p.open_phase()
            emit_l1a(p)
            p.barrier()
            p.close_phase()
            if upto != "l1a":
                p.open_phase()
                emit_l1b(p)
                p.barrier()
                p.close_phase()
            p.l1_scope.close()
            if upto in ("l1c", "full"):
                p.l1h_scope = contextlib.ExitStack()
                p.h16_sb = p.l1h_scope.enter_context(
                    p.nc.sbuf_tensor("h16", [P, KD * T], BF16))
                p.open_phase()
                emit_l1c(p)
                p.barrier()
                p.close_phase()
                p.open_phase()
                emit_l1d(p)
                p.barrier()
                p.close_phase()
                p.l1h_scope.close()
        if "x1" in p.dbg:
            @p.sec("q")
            def _(q):
                for st_ in p.streams.values():
                    st_.wait_all(q)
                p.group("dbg_x1").issue(q, p.dbg["x1"], p.x1_h)
        if "ymoe" in p.dbg:
            @p.sec("q")
            def _(q):
                for st_ in p.streams.values():
                    st_.wait_all(q)
                p.group("dbg_ymoe").issue(q, p.dbg["ymoe"], p.ymoe_h)

    if "xz" in p.dbg:
        @p.sec("q")
        def _(q):
            if "xz_st" in p.streams:
                p.streams["xz_st"].wait_all(q)
            p.group("dbg_xz").issue(q, p.dbg["xz"], p.xz_h[:, :])

    @p.sec("q")
    def _(q):
        for st_ in p.streams.values():
            st_.wait_all(q)

    blk.__exit__(None, None, None)
    p.ctx.close()
    return p


def emit_p0(p):
    x_t = p.sb("p0_x", [P, 2 * D], F32)
    xn_t = p.sb("p0_xn", [P, 2 * D], F32)
    st = p.sb("p0_stat", [P, 32], F32)

    gconst = p.group("const")

    @p.sec("q")
    def _(q):
        for o_, i_ in ((p.ident_sb[:], p.ident_c), (p.ones_sb[:], p.ones_c),
                       (p.ones16_sb[:], p.ones16_c), (p.onehotb_sb[:], p.onehotb_c),
                       (p.triuT_sb[:], p.triuT_c), (p.identb_sb[:], p.identb_c)):
            gconst.issue(q, o_, i_)

    for i in range(NT):
        par = i % 2
        xt = x_t[:, par * D:(par + 1) * D]
        nt = xn_t[:, par * D:(par + 1) * D]
        sti = st[:, par * 16:(par + 1) * 16]

        @p.sec("q")
        def _(q, i=i, xt=xt):
            if i >= 2:
                p.wait(q, f"p0_xn{i - 2}")
            p.stream("x0ld").issue(q, xt, p.x0_h[i * P:(i + 1) * P, :])

        @p.sec("v")
        def _(v, i=i):
            p.stream("x0ld").wait(v, i)
            if i >= 2:
                p.wait(v, f"p0_tr{i - 2}_1")
        p.ln_stats_apply(i, xt, nt, sti, "p0")

        if "xn_tm" in p.dbg:
            @p.sec("q")
            def _(q, i=i, nt=nt):
                p.wait(q, f"p0_xn{i}")
                p.stream("dbg_xntm").issue(q, p.dbg["xn_tm"][i * P:(i + 1) * P, :], nt)

        @p.sec("t")
        def _(t, i=i, nt=nt):
            if i == 0:
                gconst.wait_all(t)
            p.wait(t, f"p0_xn{i}")
            if "xn_tm" in p.dbg:
                p.stream("dbg_xntm").wait(t, i)
            if i >= 1:
                p.wait(t, f"p0_evB{i - 1}")
            for k in range(KD):
                ins = t.transpose(p.ps_t[:, k * P:(k + 1) * P],
                                  nt[:, k * P:(k + 1) * P], p.ident_sb[:])
                if k % 4 == 3:
                    p.mark(f"p0_tr{i}_{k // 4}", ins, "t")

        @p.sec("s")
        def _(s, i=i):
            for h2 in range(2):
                p.wait(s, f"p0_tr{i}_{h2}")
                for k in range(4):
                    kk = h2 * 4 + k
                    ins = s.activation(
                        out=p.xn_fm[:, kk * T + i * P: kk * T + (i + 1) * P],
                        in_=p.ps_t[:, kk * P:(kk + 1) * P],
                        func=AF.Copy)
                p.mark(f"p0_evA{i}" if h2 == 0 else f"p0_evB{i}", ins, "s")

    if "xn_fm" in p.dbg:
        @p.sec("q")
        def _(q):
            p.wait(q, f"p0_evB{NT - 1}")
            gd = p.group("dbg_xn")
            for k in range(KD):
                gd.issue(q, p.dbg["xn_fm"][k * P:(k + 1) * P, :],
                         p.xn_fm[:, k * T:(k + 1) * T])


def emit_router(p, stage=9):
    wr_sb = p.sb("r_wr", [P, KD * NEXP], BF16)
    brr = p.sb("r_br", [1, 2 * NEXP], F32)
    brtm = p.sb("r_brtm", [P, 2 * NEXP], F32)
    sctm = p.sb("r_sctm", [P, NT * NEXP], F32)
    scbtm = p.sb("r_scbtm", [P, NT * NEXP], F32)
    aux = p.sb("r_aux", [P, 8 * NT], F32)
    vtm = p.sb("r_vtm", [P, NT * NEXP], F32)

    gwr = p.group("r_wr")

    @p.sec("g")
    def _(g):
        for k in range(KD):
            gwr.issue(g, wr_sb[:, k * NEXP:(k + 1) * NEXP],
                      p.wr_T_h[k * P:(k + 1) * P, :])

    gbr = p.group("r_br")

    @p.sec("q")
    def _(q):
        gbr.issue(q, brr[:, 0:NEXP], p.br_h)
        gbr.issue(q, brr[:, NEXP:], p.biases_h)

    @p.sec("t")
    def _(t):
        gbr.wait_all(t)
        p.mark("r_brmm", t.matmul(p.ps_t[:, 0:2 * NEXP], p.ones_sb[:], brr[:],
                                  start=True, stop=True), "t")

    @p.sec("v")
    def _(v):
        p.wait(v, "r_brmm")
        v.tensor_copy(out=brtm[:, 0:NEXP], in_=p.ps_t[:, 0:NEXP])
        v.drain()
        p.mark("r_brtm", v.tensor_tensor(out=brtm[:, NEXP:], in0=brtm[:, 0:NEXP],
                                         in1=p.ps_t[:, NEXP:2 * NEXP], op=ALU.add), "v")

    if stage < 0.5:
        return

    NB = NT * NEXP
    brt2 = p.sb("r_brt2", [P, 2 * NB], F32)

    @p.sec("v")
    def _(v):
        p.wait(v, "r_brtm")
        for tb in range(NT):
            v.tensor_copy(out=brt2[:, tb * NEXP:(tb + 1) * NEXP], in_=brtm[:, 0:NEXP])
            v.tensor_copy(out=brt2[:, NB + tb * NEXP:NB + (tb + 1) * NEXP],
                          in_=brtm[:, NEXP:])
        p.mark("r_brt2", v.drain(), "v")

    @p.sec("t")
    def _(t):
        gwr.wait_all(t)
        for tb in range(NT):
            for k in range(KD):
                ins = t.matmul(p.ps_a[:, tb * NEXP:(tb + 1) * NEXP],
                               p.xn_fm[:, k * T + tb * P:k * T + (tb + 1) * P],
                               wr_sb[:, k * NEXP:(k + 1) * NEXP],
                               start=(k == 0), stop=(k == KD - 1))
        p.mark("r_mm", ins, "t")

    @p.sec("v")
    def _(v):
        p.wait(v, "r_brt2")
        p.wait(v, "r_mm")
        v.tensor_tensor(out=sctm[:, 0:NB], in0=p.ps_a[:, 0:NB],
                        in1=brt2[:, 0:NB], op=ALU.add)
        p.mark(f"r_ev{NT - 1}", v.tensor_tensor(
            out=scbtm[:, 0:NB], in0=p.ps_a[:, 0:NB],
            in1=brt2[:, NB:], op=ALU.add), "v")

    NB = NT * NEXP
    if stage < 2:
        return

    for tb in range(NT):
        sl = slice(tb * NEXP, (tb + 1) * NEXP)

        @p.sec("v")
        def _(v, tb=tb, sl=sl):
            if tb == 0:
                p.wait(v, f"r_ev{NT - 1}")
            sc = sctm[:, sl]
            scb = scbtm[:, sl]
            v3 = vtm[:, sl]
            m1 = aux[:, 0:1]
            m2 = aux[:, 1:2]
            mx = aux[:, 2:3]

            def b(col):
                return col.broadcast_to([P, NEXP])

            v.reduce_max(out=m1, in_=scb, axis=AX)
            v.drain()
            v.tensor_tensor(out=v3, in0=scb, in1=b(m1), op=ALU.is_ge)
            v.drain()
            v.scalar_tensor_tensor(out=v3, in0=v3, scalar=-BIG, in1=scb,
                                   op0=ALU.mult, op1=ALU.add)
            v.drain()
            v.reduce_max(out=m2, in_=v3, axis=AX)
            v.drain()
            v.tensor_tensor(out=v3, in0=scb, in1=b(m2), op=ALU.is_ge)
            v.drain()
            v.reduce_max(out=mx, in_=sc, axis=AX)
            v.drain()
            ins = v.tensor_tensor(out=sc, in0=sc, in1=b(mx), op=ALU.subtract)
            if tb == NT - 1:
                p.mark("r_sub", ins, "v")

    if stage < 3:
        return

    @p.sec("s")
    def _(s):
        p.wait(s, "r_sub")
        p.mark("r_exp", s.activation(out=scbtm[:, 0:NB], in_=sctm[:, 0:NB], func=AF.Exp), "s")

    for tb in range(NT):
        sl = slice(tb * NEXP, (tb + 1) * NEXP)

        @p.sec("v")
        def _(v, tb=tb, sl=sl):
            if tb == 0:
                p.wait(v, "r_exp")
            sc = scbtm[:, sl]
            v3 = vtm[:, sl]
            vs = aux[:, 3:4]

            def b(col):
                return col.broadcast_to([P, NEXP])

            v.tensor_tensor(out=v3, in0=v3, in1=sc, op=ALU.mult)
            v.drain()
            v.reduce_sum(out=vs, in_=v3, axis=AX)
            v.drain()
            v.reciprocal(out=vs, in_=vs)
            v.drain()
            ins = v.tensor_tensor(out=v3, in0=v3, in1=b(vs), op=ALU.mult)
            v.drain()
            if tb == NT - 1:
                p.mark("r_val", ins, "v")

    if stage < 4:
        return

    @p.sec("t")
    def _(t):
        p.wait(t, "r_val")
        p.mark("r_vtr", t.transpose(p.ps_t[:, 0:P], vtm[:, 0:P], p.ident_sb[:]), "t")

    @p.sec("v")
    def _(v):
        p.wait(v, "r_vtr")
        p.mark("r_vfm", v.tensor_copy(out=p.val_fm[:, 0:P], in_=p.ps_t[:, 0:P]), "v")

    @p.sec("q")
    def _(q):
        p.wait(q, "r_vfm")
        p.group("r_valst").issue(q, p.val_h, p.val_fm[:, 0:P])



    if "val_fm" in p.dbg:
        @p.sec("q")
        def _(q):
            p.wait(q, "r_vfm")
            p.group("dbg_val").issue(q, p.dbg["val_fm"], p.val_fm[:, 0:P])


def emit_p1(p):
    w_in_sb = p.sb("p1_win", [P, KD * 4096], BF16)
    xz_ev = p.sb("p1_ev", [P, 4 * 512], BF16)
    xz_rev = p.sb("p1_rev", [P, 4 * 512], BF16)
    TP = 512

    gwin = p.group("p1_win")

    @p.sec("g")
    def _(g):
        for k in range(KD):
            gwin.issue(g, w_in_sb[:, k * 4096:(k + 1) * 4096],
                       p.w_inT_h[k * P:(k + 1) * P, :])

    NO = 2 * DI // P
    sink = p.scrh("p1_sink", (P, 64), BF16)
    for c in range(T // TP):
        for o in range(NO):
            idx = c * NO + o
            ps = [p.ps_a, p.ps_b][o % 2]

            @p.sec("t")
            def _(t, c=c, o=o, idx=idx, ps=ps):
                if idx == 0:
                    gwin.wait_all(t)
                if idx >= 2:
                    p.wait(t, f"p1_ev{idx - 2}")
                for k in range(KD):
                    ins = t.matmul(ps[:, 0:TP],
                                   w_in_sb[:, k * 4096 + o * P:k * 4096 + (o + 1) * P],
                                   p.xn_fm[:, k * T + c * TP:k * T + (c + 1) * TP],
                                   start=(k == 0), stop=(k == KD - 1))
                p.mark(f"p1_mm{idx}", ins, "t")

            @p.sec("s")
            def _(s, idx=idx, ps=ps):
                p.wait(s, f"p1_mm{idx}")
                if idx >= 4:
                    p.stream("xz_st", 4).wait(s, idx - 4)
                    p.wait(s, f"p1_rev{idx - 4}")
                p.mark(f"p1_ev{idx}", s.activation(
                    out=xz_ev[:, (idx % 4) * TP:(idx % 4 + 1) * TP],
                    in_=ps[:, 0:TP], func=AF.Copy), "s")

            @p.sec("v")
            def _(v, idx=idx):
                p.wait(v, f"p1_ev{idx}")
                if idx >= 4:
                    p.stream("xzr_st", 4).wait(v, idx - 4)
                sl = slice((idx % 4) * TP, (idx % 4 + 1) * TP)
                p.mark(f"p1_rev{idx}", v.tensor_copy(
                    out=xz_rev[:, sl],
                    in_=xz_ev[:, (idx % 4 + 1) * TP - 1:(idx % 4) * TP - 1 if idx % 4 else None:-1]), "v")

            @p.sec("q")
            def _(q, c=c, o=o, idx=idx):
                p.wait(q, f"p1_ev{idx}")
                p.stream("xz_st", 4).issue(
                    q, p.xz_h[o * P:(o + 1) * P, c * TP:(c + 1) * TP],
                    xz_ev[:, (idx % 4) * TP:(idx % 4 + 1) * TP])
                p.wait(q, f"p1_rev{idx}")
                p.stream("xzr_st", 4).issue(
                    q, p.xzr_h[o * P:(o + 1) * P, T - (c + 1) * TP:T - c * TP],
                    xz_rev[:, (idx % 4) * TP:(idx % 4 + 1) * TP])



def emit_p2a_moe(p):
    wl1_sb = p.sb("moe_wl1", [P, KD * 4096], BF16)
    wl2_sb = p.sb("moe_wl2", [P, KDI * D], BF16)
    bl1_sb = p.sb("moe_bl1", [P, NEXP * 32], F32)
    sa16 = p.sb("moe_sa", [P, 16 * TB], BF16)
    gated = p.sb("moe_gat", [P, 16 * TB], BF16)
    btmp = p.sb("moe_btmp", [P, 2 * TB], F32)
    val_sb = p.sb("moe_val", [P, 2 * TB], F32)
    ytmp = p.sb("moe_ytmp", [P, 8 * TB], F32)

    bl2r_sb = p.sb("moe_bl2r", [NEXP, D], BF16)
    valb = [p.sb(f"moe_valb{n}", [1, T], BF16) for n in range(NEXP)]
    valblk16 = p.sb("moe_valblk", [NEXP, T], BF16)
    gb = p.group("moe_bl1")

    @p.sec("g")
    def _(g):
        p.streams["r_valst"].wait_all(g)
        gvb = p.group("moe_valblk")
        for e in range(NEXP):
            gvb.issue(g, valb[e][:], p.val_h[e::NEXP, :])
            gvb.issue(g, valblk16[e:e + 1, :], p.val_h[e::NEXP, :])

    @p.sec("q")
    def _(q):
        for n in range(NEXP):
            gb.issue(q, bl1_sb[:, n * 32:(n + 1) * 32], p.bl1_h[n])

    @p.sec("g")
    def _(g):
        gb2 = p.group("moe_bl2r")
        for n in range(NEXP):
            gb2.issue(g, bl2r_sb[n:n + 1, :], p.bl2r_h[n:n + 1, :])

    it_dma = []

    for n in range(NEXP):
        gw = p.group(f"moe_w{n}")

        @p.sec("g")
        def _(g, n=n, gw=gw):
            if n >= 1:
                p.wait(g, f"moe_mmdone{n - 1}")
            for k in range(KD):
                gw.issue(g, wl1_sb[:, k * 4096:(k + 1) * 4096],
                         p.wl1_h[n, k * P:(k + 1) * P, :])
            for k in range(KDI):
                gw.issue(g, wl2_sb[:, k * D:(k + 1) * D],
                         p.wl2_h[n, k * P:(k + 1) * P, :])

        for j in range(NTB):
            it = n * NTB + j

            @p.sec("t")
            def _(t, n=n, j=j, it=it, gw=gw):
                if j == 0:
                    gw.wait_all(t)
                    if n == 0:
                        gb.wait_all(t)
                        p.streams["moe_valblk"].wait_all(t)
                if it >= 1:
                    p.wait(t, f"moe_valcp{it - 1}")
                    p.wait(t, f"moe_gat{it - 1}_15")
                ins = t.matmul(p.ps_t[:, 0:TB], p.ones16_sb[:],
                               valb[n][:, j * TB:(j + 1) * TB],
                               start=True, stop=True)
                p.mark(f"moe_valmm{it}", ins, "t")

            @p.sec("v")
            def _(v, it=it):
                p.wait(v, f"moe_valmm{it}")
                p.mark(f"moe_valcp{it}", v.tensor_copy(
                    out=val_sb[:, 0:TB], in_=p.ps_t[:, 0:TB]), "v")

            for i in range(16):
                par = i % 2
                a_ps = [p.ps_a, p.ps_b][par]
                a_sl = slice(0, TB)
                b_sl = [slice(TB, 2 * TB), slice(2 * TB, 3 * TB)][par]

                @p.sec("t")
                def _(t, n=n, j=j, i=i, it=it, a_ps=a_ps, a_sl=a_sl, b_sl=b_sl):
                    if i >= 2:
                        p.wait(t, f"moe_sa{it}_{i - 2}")
                        p.wait(t, f"moe_gat{it}_{i - 2}")
                    elif it >= 1:
                        p.wait(t, f"moe_sa{it - 1}_{14 + i}")
                        p.wait(t, f"moe_gat{it - 1}_{14 + i}")
                    if i == 0:
                        p.wait(t, f"moe_valcp{it}")
                    for o, ps, psl, tag in ((i, a_ps, a_sl, "a"),
                                            (16 + i, p.ps_t, b_sl, "b")):
                        for k in range(KD):
                            ins = t.matmul(
                                ps[:, psl],
                                wl1_sb[:, k * 4096 + o * P:k * 4096 + (o + 1) * P],
                                p.xn_fm[:, k * T + j * TB:k * T + (j + 1) * TB],
                                start=(k == 0), stop=(k == KD - 1))
                        p.mark(f"moe_mm{tag}{it}_{i}", ins, "t")

                @p.sec("s")
                def _(s, n=n, it=it, i=i, a_ps=a_ps, a_sl=a_sl):
                    p.wait(s, f"moe_mma{it}_{i}")
                    p.mark(f"moe_sa{it}_{i}", s.activation(
                        out=sa16[:, i * TB:(i + 1) * TB],
                        in_=a_ps[:, a_sl], func=AF.Silu,
                        bias=bl1_sb[:, n * 32 + i:n * 32 + i + 1]), "s")

                @p.sec("v")
                def _(v, n=n, it=it, i=i, par=par, b_sl=b_sl):
                    p.wait(v, f"moe_mmb{it}_{i}")
                    v.scalar_tensor_tensor(
                        out=btmp[:, par * TB:(par + 1) * TB],
                        in0=p.ps_t[:, b_sl],
                        scalar=bl1_sb[:, n * 32 + 16 + i:n * 32 + 17 + i],
                        in1=val_sb[:, 0:TB], op0=ALU.add, op1=ALU.mult)
                    p.wait(v, f"moe_sa{it}_{i}")
                    p.mark(f"moe_gat{it}_{i}", v.tensor_tensor(
                        out=gated[:, i * TB:(i + 1) * TB],
                        in0=sa16[:, i * TB:(i + 1) * TB],
                        in1=btmp[:, par * TB:(par + 1) * TB], op=ALU.mult), "v")

            @p.sec("t")
            def _(t, n=n, j=j, it=it):
                p.wait(t, f"moe_gat{it}_15")
                if it >= 1:
                    p.wait(t, f"moe_yev{it - 1}")
                if n == 0:
                    p.streams["moe_bl2r"].wait_all(t)
                for o in range(8):
                    for k in range(KDI):
                        ins = t.matmul(
                            p.ps_y[:, o * TB:(o + 1) * TB],
                            wl2_sb[:, k * D + o * P:k * D + (o + 1) * P],
                            gated[:, k * TB:(k + 1) * TB],
                            start=(k == 0),
                            stop=(k == KDI - 1 and n != 0))
                    if n == 0:
                        ins = t.matmul(
                            p.ps_y[:, o * TB:(o + 1) * TB],
                            bl2r_sb[:, o * P:(o + 1) * P],
                            valblk16[0:NEXP, j * TB:(j + 1) * TB],
                            start=False, stop=True)
                p.mark(f"moe_ymm{it}", ins, "t")
                if j == NTB - 1:
                    p.mark(f"moe_mmdone{n}", t.drain(), "t")

            @p.sec("s")
            def _(s, it=it):
                p.wait(s, f"moe_ymm{it}")
                if it >= 1:
                    st_, use_ = it_dma[it - 1]
                    st_.wait(s, use_)
                for o in range(8):
                    ins = s.activation(
                        out=ytmp[:, o * TB:(o + 1) * TB],
                        in_=p.ps_y[:, o * TB:(o + 1) * TB], func=AF.Copy)
                p.mark(f"moe_yev{it}", s.drain(), "s")

            @p.sec("g")
            def _(g, n=n, j=j, it=it):
                p.wait(g, f"moe_yev{it}")
                st_ = p.stream(f"yac{j}", 1)
                use_ = st_.n
                dst = p.ymoe_h.rearrange("pp (o t) -> pp o t", t=T)[:, :, j * TB:(j + 1) * TB]
                st_.issue(g, dst,
                    ytmp.ap().rearrange("pp (o t) -> pp o t", o=8),
                    accum_op=(ALU.bypass if n == 0 else ALU.add))
                it_dma.append((st_, use_))



def emit_p2b(p):
    HAL = 264
    WXW = 112
    wxs = p.sb("mb_wx", [P, 2 * KDI * WXW], BF16)
    wds = p.sb("mb_wd", [DTR, 2 * DI], BF16)
    wos = p.sb("mb_wo", [P, KDI * D], BF16)
    cwv = p.sb("mb_cw", [P, 2 * KDI * 4], F32)
    cbv = p.sb("mb_cb", [P, 2 * KDI], F32)
    bdv = p.sb("mb_bd", [P, 2 * KDI], F32)
    dsv = p.sb("mb_ds", [P, 2 * KDI], F32)
    alg = p.sb("mb_al", [P, 2 * KDI * NST], F32)
    asb = p.sb("mb_A", [P, 2 * KDI * NST], F32)
    xzb = p.sb("mb_xzb", [P, 32 * HAL], BF16)
    x16 = p.sb("mb_x16", [P, KDI * TC], BF16)
    u16 = p.sb("mb_u16", [P, KDI * TC], BF16)
    dtb = p.sb("mb_dt", [P, KDI * TC], F32)
    sc1 = p.sb("mb_sc1", [P, KDI * TC], F32)
    dbu = p.sb("mb_dbu", [P, KDI * TC], F32)
    hbf = p.sb("mb_h", [P, KDI * TC], F32)
    yac = p.sb("mb_y", [P, KDI * TC], F32)
    xd16 = p.sb("mb_xd", [DTR, TC], BF16)
    xB16 = p.sb("mb_xB", [NST, TC], BF16)
    xC16 = p.sb("mb_xC", [NST, TC], BF16)
    Bb = p.sb("mb_Bb", [P, TC], F32)
    Cb = p.sb("mb_Cb", [P, TC], F32)
    car = p.sb("mb_car", [P, NST * KDI], F32)
    o16 = p.sb("mb_o16", [P, KDI * TC], BF16)
    sbb = p.sb("mb_sbb", [P, 8 * TC], F32)
    gld = p.sb("mb_gld", [P, 8 * TC], F32)
    x1t = p.sb("mb_x1t", [P, 2 * D], F32)

    gw = p.group("mb_w")

    @p.sec("g")
    def _(g):
        for d_ in range(2):
            for k in range(KDI):
                gw.issue(g, wxs[:, (d_ * KDI + k) * WXW:(d_ * KDI + k + 1) * WXW],
                         p.wxT_h[d_, k * P:(k + 1) * P, :])
            gw.issue(g, wds[:, d_ * DI:(d_ + 1) * DI], p.wdtT_h[d_])
        for k in range(KDI):
            gw.issue(g, wos[:, k * D:(k + 1) * D], p.w_outT_h[k * P:(k + 1) * P, :])

    gc = p.group("mb_c")

    @p.sec("q")
    def _(q):
        for o_, i_ in ((cwv, p.conv_w_h), (cbv, p.conv_b_h), (bdv, p.bdt_h),
                       (dsv, p.d_skip_h), (alg, p.a_log_h)):
            gc.issue(q, o_[:, 0:o_.shape[1] // 2], i_[0])
            gc.issue(q, o_[:, o_.shape[1] // 2:], i_[1])

    @p.sec("s")
    def _(s):
        gc.wait_all(s)
        p.mark("mb_expA", s.activation(out=asb[:], in_=alg[:], func=AF.Exp), "s")

    @p.sec("v")
    def _(v):
        p.wait(v, "mb_expA")
        p.mark("mb_A", v.tensor_scalar(out=asb[:], in0=asb[:], scalar1=-1.0,
                                       scalar2=None, op0=ALU.mult), "v")

    for d_ in (1, 0):
        xsrc = p.xzr_h if d_ == 1 else p.xz_h

        @p.sec("v")
        def _(v, d_=d_):
            if d_ == 1:
                p.wait(v, "mb_A")
            p.mark(f"mb_car0_{d_}", v.memset(car[:], 0.0), "v")

        for ci in range(NCH):
            cid = f"{d_}_{ci}"
            lo = ci * TC - 3

            if ci == 0:
                @p.sec("v")
                def _(v, d_=d_, cid=cid):
                    if d_ == 0:
                        p.wait(v, f"mb_z{1}_{NCH - 1}")
                    ins = v.memset(xzb.ap().rearrange(
                        "p (dt c) -> p dt c", c=HAL)[:, :, 0:3], 0.0)
                    p.mark(f"mb_halo{cid}", ins, "v")

            @p.sec("q")
            def _(q, d_=d_, ci=ci, cid=cid, lo=lo, xsrc=xsrc):
                st_ = p.stream("mb_xzld", 2)
                if ci == 0:
                    p.wait(q, f"mb_halo{cid}")
                else:
                    p.wait(q, f"mb_conv{d_}_{ci - 1}")
                    p.wait(q, f"mb_z{d_}_{ci - 1}")
                for dt_ in range(32):
                    src_lo = max(lo, 0)
                    ncol = (ci + 1) * TC - src_lo
                    off = 3 if ci == 0 else 0
                    st_.issue(q, xzb[:, dt_ * HAL + off:dt_ * HAL + off + ncol],
                              xsrc[dt_ * P:(dt_ + 1) * P, src_lo:(ci + 1) * TC])

            @p.sec("v")
            def _(v, d_=d_, ci=ci, cid=cid):
                st_ = p.stream("mb_xzld", 2)
                st_.wait_all(v)
                if ci >= 1 or d_ == 0:
                    prev = f"{d_}_{ci - 1}" if ci >= 1 else f"1_{NCH - 1}"
                    p.wait(v, f"mb_woutdone{prev}")
                for dt_ in range(KDI):
                    base = dt_ * HAL
                    cacc = dtb[:, dt_ * TC:(dt_ + 1) * TC]
                    v.tensor_scalar(
                        out=cacc, in0=xzb[:, base:base + TC],
                        scalar1=cwv[:, (d_ * KDI + dt_) * 4:(d_ * KDI + dt_) * 4 + 1],
                        scalar2=cbv[:, d_ * KDI + dt_:d_ * KDI + dt_ + 1],
                        op0=ALU.mult, op1=ALU.add)
                    for j in (1, 2, 3):
                        v.scalar_tensor_tensor(
                            out=cacc, in0=xzb[:, base + j:base + j + TC],
                            scalar=cwv[:, (d_ * KDI + dt_) * 4 + j:(d_ * KDI + dt_) * 4 + j + 1],
                            in1=cacc, op0=ALU.mult, op1=ALU.add)
                p.mark(f"mb_conv{cid}", v.drain(), "v")

            @p.sec("s")
            def _(s, cid=cid):
                p.wait(s, f"mb_conv{cid}")
                for dt_ in range(KDI):
                    s.activation(out=x16[:, dt_ * TC:(dt_ + 1) * TC],
                                 in_=dtb[:, dt_ * TC:(dt_ + 1) * TC], func=AF.Silu)
                p.mark(f"mb_x16{cid}", s.drain(), "s")

            @p.sec("t")
            def _(t, d_=d_, ci=ci, cid=cid):
                if d_ == 1 and ci == 0:
                    gw.wait_all(t)
                p.wait(t, f"mb_x16{cid}")
                if ci >= 1 or d_ == 0:
                    p.wait(t, f"mb_xdblev{d_ if ci else 1}_{ci - 1 if ci else NCH - 1}")
                for k in range(KDI):
                    ins = t.matmul(p.ps_a[0:112, 0:TC],
                                   wxs[:, (d_ * KDI + k) * WXW:(d_ * KDI + k + 1) * WXW],
                                   x16[:, k * TC:(k + 1) * TC],
                                   start=(k == 0), stop=(k == KDI - 1))
                p.mark(f"mb_xdbl{cid}", ins, "t")

            @p.sec("s")
            def _(s, cid=cid):
                p.wait(s, f"mb_xdbl{cid}")
                s.activation(out=xd16[:], in_=p.ps_a[0:DTR, 0:TC], func=AF.Copy)
                s.activation(out=xB16[:], in_=p.ps_a[64:80, 0:TC], func=AF.Copy)
                s.activation(out=xC16[:], in_=p.ps_a[96:112, 0:TC], func=AF.Copy)
                p.mark(f"mb_xdblev{cid}", s.drain(), "s")

            for oi in range(KDI):
                @p.sec("t")
                def _(t, d_=d_, ci=ci, cid=cid, oi=oi):
                    if oi == 0:
                        p.wait(t, f"mb_xdblev{cid}")
                    if oi >= 2:
                        p.wait(t, f"mb_spv{cid}_{oi - 2}")
                    ps = [p.ps_a, p.ps_b][oi % 2]
                    p.mark(f"mb_dtmm{cid}_{oi}", t.matmul(
                        ps[:, 0:TC], wds[:, d_ * DI + oi * P:d_ * DI + (oi + 1) * P],
                        xd16[:], start=True, stop=True), "t")

                @p.sec("s")
                def _(s, d_=d_, ci=ci, cid=cid, oi=oi):
                    p.wait(s, f"mb_dtmm{cid}_{oi}")
                    if oi == 0 and (ci >= 1 or d_ == 0):
                        prev = f"{d_}_{ci - 1}" if ci >= 1 else f"1_{NCH - 1}"
                        p.wait(s, f"mb_scan{prev}_{NST - 1}")
                    ps = [p.ps_a, p.ps_b][oi % 2]
                    p.mark(f"mb_sps{cid}_{oi}", s.activation(
                        out=sc1[:, oi * TC:(oi + 1) * TC], in_=ps[:, 0:TC],
                        func=AF.Identity,
                        bias=bdv[:, d_ * KDI + oi:d_ * KDI + oi + 1]), "s")

                @p.sec("v")
                def _(v, cid=cid, oi=oi):
                    p.wait(v, f"mb_sps{cid}_{oi}")
                    xb = sc1[:, oi * TC:(oi + 1) * TC]
                    r = dtb[:, oi * TC:(oi + 1) * TC]
                    v.tensor_scalar_max(r, xb, 0.0)
                    v.drain()
                    p.mark(f"mb_spv{cid}_{oi}", v.scalar_tensor_tensor(
                        out=xb, in0=r, scalar=-2.0, in1=xb,
                        op0=ALU.mult, op1=ALU.add), "v")

            @p.sec("s")
            def _(s, cid=cid):
                p.wait(s, f"mb_spv{cid}_{KDI - 1}")
                s.activation(out=sc1[:], in_=sc1[:], func=AF.Exp)
                s.drain()
                p.mark(f"mb_spl{cid}", s.activation(
                    out=sc1[:], in_=sc1[:], func=AF.Ln, bias=1.0), "s")

            @p.sec("v")
            def _(v, cid=cid):
                p.wait(v, f"mb_spl{cid}")
                v.tensor_add(dtb[:], dtb[:], sc1[:])
                v.drain()
                p.mark(f"mb_u{cid}", v.tensor_tensor(
                    out=u16[:], in0=dtb[:], in1=x16[:], op=ALU.mult), "v")

            for n in range(NST):
                nid = f"{cid}_{n}"

                @p.sec("t")
                def _(t, cid=cid, nid=nid, n=n):
                    if n == 0:
                        p.wait(t, f"mb_u{cid}")
                    else:
                        p.wait(t, f"mb_bc_cp{cid}_{n - 1}")
                    t.matmul(p.ps_t[:, 0:TC], p.onehotb_sb[0:NST, n * P:(n + 1) * P],
                             xB16[:], start=True, stop=True)
                    p.mark(f"mb_bc{nid}", t.matmul(
                        p.ps_t[:, 2 * TC:3 * TC], p.onehotb_sb[0:NST, n * P:(n + 1) * P],
                        xC16[:], start=True, stop=True), "t")

                @p.sec("v")
                def _(v, nid=nid, cid=cid, n=n):
                    p.wait(v, f"mb_bc{nid}")
                    v.tensor_copy(out=Bb[:], in_=p.ps_t[:, 0:TC])
                    p.mark(f"mb_bc_cp{cid}_{n}", v.tensor_copy(
                        out=Cb[:], in_=p.ps_t[:, 2 * TC:3 * TC]), "v")

                @p.sec("s")
                def _(s, d_=d_, nid=nid, cid=cid, n=n):
                    if n == 0:
                        p.wait(s, f"mb_u{cid}")
                    else:
                        p.wait(s, f"mb_scan{cid}_{n - 1}")
                    for dt_ in range(KDI):
                        s.activation(out=sc1[:, dt_ * TC:(dt_ + 1) * TC],
                                     in_=dtb[:, dt_ * TC:(dt_ + 1) * TC], func=AF.Exp,
                                     scale=asb[:, (d_ * KDI + dt_) * NST + n:
                                               (d_ * KDI + dt_) * NST + n + 1])
                    p.mark(f"mb_dA{nid}", s.drain(), "s")

                @p.sec("g")
                def _(g, nid=nid, cid=cid, n=n):
                    p.wait(g, f"mb_bc_cp{cid}_{n}")
                    if n == 0:
                        p.wait(g, f"mb_u{cid}")
                    else:
                        p.wait(g, f"mb_scan{cid}_{n - 1}")
                    p.mark(f"mb_dbu{nid}", g.tensor_tensor(
                        out=dbu.ap().rearrange("p (d t) -> p d t", t=TC),
                        in0=u16.ap().rearrange("p (d t) -> p d t", t=TC),
                        in1=Bb.ap().unsqueeze(1).broadcast_to([P, KDI, TC]),
                        op=ALU.mult), "g")

                @p.sec("v")
                def _(v, nid=nid, cid=cid, n=n):
                    p.wait(v, f"mb_dA{nid}")
                    p.wait(v, f"mb_dbu{nid}")
                    for dt_ in range(KDI):
                        v.tensor_tensor_scan(
                            out=hbf[:, dt_ * TC:(dt_ + 1) * TC],
                            data0=sc1[:, dt_ * TC:(dt_ + 1) * TC],
                            data1=dbu[:, dt_ * TC:(dt_ + 1) * TC],
                            initial=car[:, n * KDI + dt_:n * KDI + dt_ + 1],
                            op0=ALU.mult, op1=ALU.add)
                    v.drain()
                    v.tensor_copy(out=car[:, n * KDI:(n + 1) * KDI],
                                  in_=hbf[:, TC - 1::TC])
                    v.drain()
                    if n == 0:
                        v.tensor_tensor(
                            out=yac.ap().rearrange("p (d t) -> p d t", t=TC),
                            in0=hbf.ap().rearrange("p (d t) -> p d t", t=TC),
                            in1=Cb.ap().unsqueeze(1).broadcast_to([P, KDI, TC]),
                            op=ALU.mult)
                    else:
                        v.tensor_tensor(
                            out=hbf.ap().rearrange("p (d t) -> p d t", t=TC),
                            in0=hbf.ap().rearrange("p (d t) -> p d t", t=TC),
                            in1=Cb.ap().unsqueeze(1).broadcast_to([P, KDI, TC]),
                            op=ALU.mult)
                        v.drain()
                        v.tensor_add(yac[:], yac[:], hbf[:])
                    p.mark(f"mb_scan{cid}_{n}", v.drain(), "v")

            @p.sec("s")
            def _(s, cid=cid):
                p.wait(s, f"mb_scan{cid}_{NST - 1}")
                for dt_ in range(KDI):
                    s.activation(out=dbu[:, dt_ * TC:(dt_ + 1) * TC],
                                 in_=xzb[:, (16 + dt_) * HAL + 3:(16 + dt_) * HAL + 3 + TC],
                                 func=AF.Silu)
                p.mark(f"mb_z{cid}", s.drain(), "s")

            @p.sec("v")
            def _(v, d_=d_, cid=cid):
                p.wait(v, f"mb_z{cid}")
                for dt_ in range(KDI):
                    sl = slice(dt_ * TC, (dt_ + 1) * TC)
                    v.scalar_tensor_tensor(
                        out=sc1[:, sl], in0=x16[:, sl],
                        scalar=dsv[:, d_ * KDI + dt_:d_ * KDI + dt_ + 1],
                        in1=yac[:, sl], op0=ALU.mult, op1=ALU.add)
                v.drain()
                p.mark(f"mb_o16{cid}", v.tensor_tensor(
                    out=o16[:], in0=sc1[:], in1=dbu[:], op=ALU.mult), "v")

            @p.sec("t")
            def _(t, d_=d_, ci=ci, cid=cid):
                p.wait(t, f"mb_o16{cid}")
                if d_ == 1 and ci >= 1:
                    p.wait(t, f"mb_sev{d_}_{ci - 1}")
                elif d_ == 0:
                    p.wait(t, f"mb_x1{d_}_{ci - 1}" if ci >= 1 else f"mb_sev1_{NCH - 1}")
                for o in range(KD):
                    for k in range(KDI):
                        ins = t.matmul(p.ps_y[:, o * TC:(o + 1) * TC],
                                       wos[:, k * D + o * P:k * D + (o + 1) * P],
                                       o16[:, k * TC:(k + 1) * TC],
                                       start=(k == 0), stop=(k == KDI - 1))
                p.mark(f"mb_smm{cid}", ins, "t")

            if d_ == 1:
                @p.sec("v")
                def _(v, ci=ci, cid=cid):
                    p.wait(v, f"mb_smm{cid}")
                    if ci >= 2:
                        p.stream("mb_sbst", 2).wait(v, ci - 2)
                    for o in range(KD):
                        v.tensor_copy(
                            out=sbb[:, o * TC:(o + 1) * TC],
                            in_=p.ps_y[:, (o + 1) * TC - 1:o * TC - 1 if o else None:-1])
                    p.mark(f"mb_sev{cid}", v.drain(), "v")

                @p.sec("q")
                def _(q, ci=ci, cid=cid):
                    p.wait(q, f"mb_sev{cid}")
                    dst = p.sb_h.rearrange("pp (o t) -> pp o t", t=T)[
                        :, :, T - (ci + 1) * TC:T - ci * TC]
                    p.stream("mb_sbst", 2).issue(
                        q, dst, sbb.ap().rearrange("p (o t) -> p o t", o=KD))
            else:
                @p.sec("q")
                def _(q, ci=ci, cid=cid):
                    st1 = p.stream("mb_sbld", 2)
                    st2 = p.stream("mb_gld", 2)
                    if ci == 0:
                        p.streams["mb_sbst"].wait_all(q)
                        for st3 in [p.streams.get(f"yac{j}") for j in range(NTB)]:
                            if st3 is not None:
                                st3.wait_all(q)
                    if ci >= 1:
                        p.wait(q, f"mb_gate{d_}_{ci - 1}")
                    st1.issue(q, sbb.ap().rearrange("p (o t) -> p o t", o=KD),
                              p.sb_h.rearrange("pp (o t) -> pp o t", t=T)[
                                  :, :, ci * TC:(ci + 1) * TC])
                    st2.issue(q, gld.ap().rearrange("p (o t) -> p o t", o=KD),
                              p.ymoe_h.rearrange("pp (o t) -> pp o t", t=T)[
                                  :, :, ci * TC:(ci + 1) * TC])
                    st3 = p.stream("mb_x0ld", 2)
                    if ci >= 1:
                        p.stream("mb_x1st", 2).wait(q, ci - 1)
                    st3.issue(q, x1t.ap().rearrange("pp (a dd) -> pp a dd", a=2),
                              p.x0_h[ci * TC:(ci + 1) * TC, :].rearrange(
                                  "(a pp) dd -> pp a dd", pp=P))

                @p.sec("s")
                def _(s, ci=ci, cid=cid):
                    p.stream("mb_gld", 2).wait(s, ci)
                    p.mark(f"mb_g{cid}", s.activation(
                        out=gld[:], in_=gld[:], func=AF.Silu), "s")

                @p.sec("v")
                def _(v, ci=ci, cid=cid):
                    p.wait(v, f"mb_smm{cid}")
                    p.stream("mb_sbld", 2).wait(v, ci)
                    p.wait(v, f"mb_g{cid}")
                    v.tensor_add(sbb[:], p.ps_y[:, 0:KD * TC], sbb[:])
                    v.drain()
                    v.scalar_tensor_tensor(out=sbb[:], in0=sbb[:], scalar=0.5,
                                           in1=gld[:], op0=ALU.mult, op1=ALU.mult)
                    p.mark(f"mb_sg{cid}", v.drain(), "v")

                @p.sec("t")
                def _(t, ci=ci, cid=cid):
                    p.wait(t, f"mb_sg{cid}")
                    if ci >= 1:
                        p.wait(t, f"mb_x1{d_}_{ci - 1}")
                    for h in range(2):
                        for o in range(KD):
                            idx = h * KD + o
                            ins = t.transpose(
                                p.ps_y[:, idx * P:(idx + 1) * P],
                                sbb[:, o * TC + h * P:o * TC + (h + 1) * P],
                                p.ident_sb[:])
                            if idx % 8 == 7:
                                p.mark(f"mb_str{cid}_{idx // 8}", ins, "t")

                @p.sec("v")
                def _(v, ci=ci, cid=cid):
                    p.stream("mb_x0ld", 2).wait(v, ci)
                    for h in range(2):
                        p.wait(v, f"mb_str{cid}_{h}")
                    for h in range(2):
                        for o in range(KD):
                            idx = h * KD + o
                            v.tensor_tensor(
                                out=x1t[:, h * D + o * P:h * D + (o + 1) * P],
                                in0=p.ps_y[:, idx * P:(idx + 1) * P],
                                in1=x1t[:, h * D + o * P:h * D + (o + 1) * P],
                                op=ALU.add)
                    p.mark(f"mb_x1{cid}", v.drain(), "v")

                @p.sec("q")
                def _(q, ci=ci, cid=cid):
                    p.wait(q, f"mb_x1{cid}")
                    st_ = p.stream("mb_x1st", 2)
                    st_.issue(q, p.x1_h[ci * TC:(ci + 1) * TC, :].rearrange(
                        "(a pp) dd -> pp a dd", pp=P),
                        x1t.ap().rearrange("pp (a dd) -> pp a dd", a=2))
                    q.sem_inc(p.sem["q"], 1)
                    p.cnt["q"] += 1
                    p.ev.setdefault(f"mb_gate{cid}", []).append(("q", p.cnt["q"]))

            @p.sec("t")
            def _(t, d_=d_, ci=ci, cid=cid):
                p.mark(f"mb_woutdone{cid}", t.drain(), "t")

        p.barrier()



def emit_l1a(p):
    x1t = p.sb("la_x1", [P, 2 * D], F32)
    xnt = p.sb("la_xn", [P, 2 * D], F32)
    lnt = p.sb("la_ln", [P, 2 * D], F32)
    st = p.sb("la_st", [P, 32], F32)
    lg = p.sb("la_lg", [P, KD], F32)
    lb = p.sb("la_lb", [P, KD], F32)
    fb = p.sb("la_fb", [P, KD], F32)
    wf = p.sb("la_wf", [P, KD * D], BF16)
    vr = p.sb("la_vr", [P, KD * 32], BF16)
    fc = p.sb("la_fc", [32, D], BF16)
    u16 = p.sb("la_u16", [32, TC], BF16)
    gsl = p.sb("la_gsl", [P, KD * TC], BF16)

    gw = p.group("la_w")

    @p.sec("g")
    def _(g):
        for k in range(KD):
            gw.issue(g, wf[:, k * D:(k + 1) * D], p.fw_h[k * P:(k + 1) * P, :])
            gw.issue(g, vr[:, k * 32:(k + 1) * 32], p.fvr_h[k * P:(k + 1) * P, :])
        gw.issue(g, fc[:], p.fcT_h)

    gc = p.group("la_c")

    @p.sec("q")
    def _(q):
        gc.issue(q, lg[:], p.ln1_g_h)
        gc.issue(q, lb[:], p.ln1_b_h)
        gc.issue(q, fb[:], p.fb_h)

    for ci in range(NCH):
        cid = f"l1a{ci}"

        @p.sec("q")
        def _(q, ci=ci):
            st_ = p.stream("la_x1ld", 2)
            if ci >= 1:
                p.wait(q, f"la_apl{ci - 1}")
            st_.issue(q, x1t.ap().rearrange("pp (a dd) -> pp a dd", a=2),
                      p.x1_h[ci * TC:(ci + 1) * TC, :].rearrange(
                          "(a pp) dd -> pp a dd", pp=P))

        for h in range(2):
            i = 2 * ci + h

            @p.sec("v")
            def _(v, ci=ci, i=i):
                p.stream("la_x1ld", 2).wait(v, ci)
                if i >= 2:
                    p.wait(v, f"la_tr{i - 2}_3")
            p.ln_stats_apply(i, x1t[:, h * D:(h + 1) * D],
                             xnt[:, h * D:(h + 1) * D], st[:, 0:16], "la1")
            p.ln_stats_apply(i, xnt[:, h * D:(h + 1) * D],
                             lnt[:, h * D:(h + 1) * D], st[:, 16:32], "la2")

            @p.sec("t")
            def _(t, i=i, h=h):
                p.wait(t, f"la2_xn{i}")
                if i >= 1:
                    p.wait(t, f"la_ev{i - 1}")
                for k in range(KD):
                    ins = t.transpose(p.ps_y[:, k * P:(k + 1) * P],
                                      xnt[:, h * D + k * P:h * D + (k + 1) * P],
                                      p.ident_sb[:])
                    if k == KD - 1:
                        p.mark(f"la_tr{i}_0", ins, "t")
                for k in range(KD):
                    ins = t.transpose(p.ps_y[:, (KD + k) * P:(KD + k + 1) * P],
                                      lnt[:, h * D + k * P:h * D + (k + 1) * P],
                                      p.ident_sb[:])
                    if k == KD - 1:
                        p.mark(f"la_tr{i}_3", ins, "t")

            @p.sec("s")
            def _(s, i=i):
                p.wait(s, f"la_tr{i}_0")
                for k in range(KD):
                    s.activation(out=p.xn1_fm_sb[:, k * T + i * P:k * T + (i + 1) * P],
                                 in_=p.ps_y[:, k * P:(k + 1) * P], func=AF.Copy)
                p.mark(f"la_sev{i}", s.drain(), "s")

            @p.sec("v")
            def _(v, i=i):
                p.wait(v, f"la_tr{i}_3")
                if i == 0:
                    gc.wait_all(v)
                for k in range(KD):
                    v.tensor_scalar(
                        out=p.ln1_fm_sb[:, k * T + i * P:k * T + (i + 1) * P],
                        in0=p.ps_y[:, (KD + k) * P:(KD + k + 1) * P],
                        scalar1=lg[:, k:k + 1], scalar2=lb[:, k:k + 1],
                        op0=ALU.mult, op1=ALU.add)
                p.wait(v, f"la_sev{i}")
                p.mark(f"la_ev{i}", v.drain(), "v")
                if i % 2 == 1:
                    p.mark(f"la_apl{i // 2}", v.drain(), "v")

        @p.sec("t")
        def _(t, ci=ci, cid=cid):
            if ci == 0:
                gw.wait_all(t)
            p.wait(t, f"la_ev{2 * ci + 1}")
            if ci >= 1:
                p.wait(t, f"la_uev{ci - 1}")
            for k in range(KD):
                ins = t.matmul(p.ps_t[0:32, 0:TC], vr[:, k * 32:(k + 1) * 32],
                               p.xn1_fm_sb[:, k * T + ci * TC:k * T + (ci + 1) * TC],
                               start=(k == 0), stop=(k == KD - 1))
            p.mark(f"la_umm{cid}", ins, "t")

        @p.sec("s")
        def _(s, ci=ci, cid=cid):
            p.wait(s, f"la_umm{cid}")
            p.mark(f"la_uev{ci}", s.activation(
                out=u16[:], in_=p.ps_t[0:32, 0:TC], func=AF.Copy), "s")

        for o in range(KD):
            @p.sec("t")
            def _(t, ci=ci, cid=cid, o=o):
                if o == 0:
                    p.wait(t, f"la_uev{ci}")
                ps = [p.ps_a, p.ps_b][o % 2]
                if o >= 2:
                    p.wait(t, f"la_g1s{cid}_{o - 2}")
                elif ci >= 1:
                    p.wait(t, f"la_g1s" + f"l1a{ci - 1}" + f"_{KD - 2 + o}")
                for k in range(KD):
                    t.matmul(ps[:, 0:TC], wf[:, k * D + o * P:k * D + (o + 1) * P],
                             p.xn1_fm_sb[:, k * T + ci * TC:k * T + (ci + 1) * TC],
                             start=(k == 0), stop=False)
                ins = t.matmul(ps[:, 0:TC], fc[:, o * P:(o + 1) * P], u16[:],
                               start=False, stop=True)
                p.mark(f"la_g1mm{cid}_{o}", ins, "t")

            @p.sec("s")
            def _(s, ci=ci, cid=cid, o=o):
                p.wait(s, f"la_g1mm{cid}_{o}")
                if o == 0 and ci >= 1:
                    p.stream("la_g1st", 2).wait(s, ci - 1)
                ps = [p.ps_a, p.ps_b][o % 2]
                p.mark(f"la_g1s{cid}_{o}", s.activation(
                    out=gsl[:, o * TC:(o + 1) * TC], in_=ps[:, 0:TC],
                    func=AF.Silu, bias=fb[:, o:o + 1]), "s")

        @p.sec("q")
        def _(q, ci=ci, cid=cid):
            p.wait(q, f"la_g1s{cid}_{KD - 1}")
            p.stream("la_g1st", 2).issue(
                q, p.g1_h.rearrange("pp (o t) -> pp o t", t=T)[
                    :, :, ci * TC:(ci + 1) * TC],
                gsl.ap().rearrange("pp (o t) -> pp o t", o=KD))


def emit_l1b(p):
    wqkv = p.sb("lb_wqkv", [P, 3 * KD * 64], BF16)
    wo = p.sb("lb_wo", [P, KD * D], BF16)
    q16 = p.sb("lb_q", [64, T], BF16)
    k16 = p.sb("lb_k", [64, T], BF16)
    ktm = p.sb("lb_ktm", [P, NACH * 64], BF16)
    vtm = p.sb("lb_vtm", [P, NACH * 64], BF16)
    sc16 = p.sb("lb_sc", [P, P], BF16)
    mt16 = p.sb("lb_mt", [64, 64], BF16)
    x1ic = p.sb("lb_x1i", [P, KD * TC], F32)
    yall = p.sb("lb_yall", [P, KD * T], BF16)

    gwo = p.group("lb_wo")

    @p.sec("g")
    def _(g):
        for k in range(KD):
            gwo.issue(g, wo[:, k * D:(k + 1) * D], p.wo_h[k * P:(k + 1) * P, :])

    for hh in range(HEADS):
        hid_ = f"h{hh}"
        gq = p.group(f"lb_w{hh}")

        @p.sec("g")
        def _(g, hh=hh, gq=gq):
            if hh >= 1:
                p.wait(g, f"lb_proj{hh - 1}")
            for k in range(KD):
                gq.issue(g, wqkv[:, k * 64:(k + 1) * 64],
                         p.wq_h[k * P:(k + 1) * P, hh * 64:(hh + 1) * 64])
                gq.issue(g, wqkv[:, (KD + k) * 64:(KD + k + 1) * 64],
                         p.wk_h[k * P:(k + 1) * P, hh * 64:(hh + 1) * 64])
                gq.issue(g, wqkv[:, (2 * KD + k) * 64:(2 * KD + k + 1) * 64],
                         p.wv_h[k * P:(k + 1) * P, hh * 64:(hh + 1) * 64])

        for which in range(2):
            @p.sec("t")
            def _(t, hh=hh, gq=gq, hid_=hid_, which=which):
                if which == 0:
                    gq.wait_all(t)
                    if hh >= 1:
                        p.wait(t, f"lb_yev{f'h{hh - 1}'}_{NACH - 1}")
                if which >= 1:
                    for sc_ in range(4):
                        p.wait(t, f"lb_pe{hid_}_{sc_}_{which - 1}")
                for sc_ in range(4):
                    for k in range(KD):
                        ins = t.matmul(
                            p.ps_y[0:64, sc_ * 512:(sc_ + 1) * 512],
                            wqkv[:, (which * KD + k) * 64:(which * KD + k + 1) * 64],
                            p.ln1_fm_sb[:, k * T + sc_ * 512:k * T + (sc_ + 1) * 512],
                            start=(k == 0), stop=(k == KD - 1))
                    p.mark(f"lb_p{hid_}_{sc_}_{which}", ins, "t")

            for sc_ in range(4):
                @p.sec("s")
                def _(s, hid_=hid_, sc_=sc_, which=which):
                    p.wait(s, f"lb_p{hid_}_{sc_}_{which}")
                    dst = [q16, k16][which][:, sc_ * 512:(sc_ + 1) * 512]
                    ins = s.activation(out=dst, in_=p.ps_y[0:64, sc_ * 512:(sc_ + 1) * 512],
                                       func=AF.Copy)
                    p.mark(f"lb_pe{hid_}_{sc_}_{which}", ins, "s")

        for half in range(2):
            for grp in range(2):
                bank = half * 2 + grp

                @p.sec("t")
                def _(t, hid_=hid_, half=half, grp=grp, bank=bank):
                    p.wait(t, f"lb_pe{hid_}_{bank}_1")
                    for c8 in range(8):
                        c = grp * 8 + c8
                        for k in range(KD):
                            ins = t.matmul(
                                p.ps_y[:, bank * 512 + c8 * 64:bank * 512 + (c8 + 1) * 64],
                                p.ln1_fm_sb[:, k * T + c * ACH:k * T + (c + 1) * ACH],
                                wqkv[:, ((1 + half) * KD + k) * 64:((1 + half) * KD + k + 1) * 64],
                                start=(k == 0), stop=(k == KD - 1))
                    p.mark(f"lb_tm{hid_}_{half}_{grp}", ins, "t")

                @p.sec("s")
                def _(s, hid_=hid_, half=half, grp=grp, bank=bank):
                    p.wait(s, f"lb_tm{hid_}_{half}_{grp}")
                    p.mark(f"lb_tme{hid_}_{half}_{grp}", s.activation(
                        out=[ktm, vtm][half][:, grp * 512:(grp + 1) * 512],
                        in_=p.ps_y[:, bank * 512:(bank + 1) * 512],
                        func=AF.Copy), "s")

        @p.sec("v")
        def _(v, hid_=hid_):
            p.mark(f"lb_mt0{hid_}", v.memset(mt16[:], 0.0), "v")

        for c in range(NACH):
            ccid = f"{hid_}_{c}"
            csl = slice(c * ACH, (c + 1) * ACH)

            @p.sec("t")
            def _(t, hh=hh, c=c, ccid=ccid, csl=csl, hid_=hid_):
                if c == 0:
                    p.wait(t, f"lb_tme{hid_}_1_1")
                if c >= 1:
                    p.wait(t, f"lb_scev{hid_}_{c - 1}")
                p.mark(f"lb_sc{ccid}", t.matmul(
                    p.ps_a[:, 0:ACH], k16[:, csl], q16[:, csl],
                    start=True, stop=True), "t")

            @p.sec("v")
            def _(v, ccid=ccid, hid_=hid_, c=c):
                p.wait(v, f"lb_sc{ccid}")
                p.mark(f"lb_scev{hid_}_{c}", v.tensor_tensor(
                    out=sc16[:], in0=p.ps_a[:, 0:ACH], in1=p.triuT_sb[:],
                    op=ALU.mult), "v")

            @p.sec("t")
            def _(t, c=c, ccid=ccid, csl=csl, hid_=hid_):
                p.wait(t, f"lb_mtcp{hid_}_{c - 1}" if c >= 1 else f"lb_mt0{hid_}")
                if c >= 1:
                    p.wait(t, f"lb_yev{hid_}_{c - 1}")
                t.matmul(p.ps_t[0:64, 512:512 + ACH], mt16[:], q16[:, csl],
                         start=True, stop=False)
                p.wait(t, f"lb_scev{hid_}_{c}")
                p.mark(f"lb_y{ccid}", t.matmul(
                    p.ps_t[0:64, 512:512 + ACH], vtm[:, c * 64:(c + 1) * 64], sc16[:],
                    start=False, stop=True), "t")
                p.mark(f"lb_mtmm{ccid}", t.matmul(
                    p.ps_t[0:64, 0:64], ktm[:, c * 64:(c + 1) * 64],
                    vtm[:, c * 64:(c + 1) * 64],
                    start=(c == 0), stop=(c == NACH - 1)), "t")

            @p.sec("v")
            def _(v, ccid=ccid, hid_=hid_, c=c):
                p.wait(v, f"lb_mtmm{ccid}")
                p.mark(f"lb_mtcp{hid_}_{c}", v.tensor_copy(
                    out=mt16[:], in_=p.ps_t[0:64, 0:64]), "v")

            @p.sec("s")
            def _(s, hh=hh, c=c, ccid=ccid, hid_=hid_):
                p.wait(s, f"lb_y{ccid}")
                p.mark(f"lb_yev{hid_}_{c}", s.activation(
                    out=yall[(hh % 2) * 64:(hh % 2) * 64 + 64,
                             (hh // 2) * T + c * ACH:(hh // 2) * T + (c + 1) * ACH],
                    in_=p.ps_t[0:64, 512:512 + ACH], func=AF.Copy), "s")

        @p.sec("t")
        def _(t, hh=hh, hid_=hid_):
            p.mark(f"lb_proj{hh}", t.drain(), "t")

    for ci in range(NCH):
        cid = f"lwo{ci}"

        @p.sec("t")
        def _(t, ci=ci, cid=cid):
            if ci == 0:
                gwo.wait_all(t)
                p.wait(t, f"lb_yev{f'h{HEADS - 1}'}_{NACH - 1}")
            if ci >= 1:
                p.wait(t, f"lb_x1iev{ci - 1}")
            for o in range(KD):
                for k in range(KD):
                    ins = t.matmul(p.ps_y[:, o * TC:(o + 1) * TC],
                                   wo[:, k * D + o * P:k * D + (o + 1) * P],
                                   yall[:, k * T + ci * TC:k * T + (ci + 1) * TC],
                                   start=(k == 0), stop=(k == KD - 1))
            p.mark(f"lb_womm{cid}", ins, "t")

        @p.sec("v")
        def _(v, ci=ci, cid=cid):
            p.wait(v, f"lb_womm{cid}")
            if ci >= 2:
                p.stream("lb_x1ist", 2).wait(v, ci - 2)
            for o in range(KD):
                v.tensor_tensor(out=x1ic[:, o * TC:(o + 1) * TC],
                                in0=p.ps_y[:, o * TC:(o + 1) * TC],
                                in1=p.xn1_fm_sb[:, o * T + ci * TC:o * T + (ci + 1) * TC],
                                op=ALU.add)
            p.mark(f"lb_x1iev{ci}", v.drain(), "v")

        @p.sec("q")
        def _(q, ci=ci):
            p.wait(q, f"lb_x1iev{ci}")
            p.stream("lb_x1ist", 2).issue(
                q, p.x1i_h.rearrange("pp (o t) -> pp o t", t=T)[
                    :, :, ci * TC:(ci + 1) * TC],
                x1ic.ap().rearrange("pp (o t) -> pp o t", o=KD))


def emit_l1c(p):
    w1 = p.sb("lc_w1", [P, KD * HID], BF16)
    w2 = p.sb("lc_w2", [P, (HID // P) * D], BF16)
    b1v = p.sb("lc_b1", [P, HID // P], F32)
    b2v = p.sb("lc_b2", [P, KD], F32)
    g2v = p.sb("lc_g2", [P, KD], F32)
    bb2 = p.sb("lc_bb2", [P, KD], F32)
    h16 = p.h16_sb
    hid16 = p.sb("lc_hid", [P, (HID // P) * TC], BF16)
    x1ic = p.sb("lc_x1i", [P, KD * TC], F32)
    sqc = p.sb("lc_sq", [P, KD * TC], BF16)
    row = p.sb("lc_row", [1, 4 * TC], F32)
    onesc16 = p.sb("lc_onesc", [P, 1], BF16)
    gl = p.sb("lc_gl", [P, KD * TC], BF16)

    gc = p.group("lc_c")

    @p.sec("q")
    def _(q):
        gc.issue(q, g2v[:], p.ln2_g_h)
        gc.issue(q, bb2[:], p.ln2_b_h)
        gc.issue(q, onesc16[:], p.onesc16_c)

    for ci in range(NCH):
        cid = f"ln2{ci}"

        @p.sec("q")
        def _(q, ci=ci):
            st_ = p.stream("lc_x1ild", 2)
            if ci >= 1:
                p.wait(q, f"lc_app{ci - 1}")
            st_.issue(q, x1ic.ap().rearrange("pp (o t) -> pp o t", o=KD),
                      p.x1i_h.rearrange("pp (o t) -> pp o t", t=T)[
                          :, :, ci * TC:(ci + 1) * TC])

        @p.sec("s")
        def _(s, ci=ci, cid=cid):
            if ci == 0:
                p.streams["lb_x1ist"].wait_all(s)
            p.stream("lc_x1ild", 2).wait(s, ci)
            for o in range(KD):
                s.activation(out=sqc[:, o * TC:(o + 1) * TC],
                             in_=x1ic[:, o * TC:(o + 1) * TC], func=AF.Square)
            p.mark(f"lc_sq{cid}", s.drain(), "s")

        @p.sec("v")
        def _(v, ci=ci, cid=cid):
            p.stream("lc_x1ild", 2).wait(v, ci)
            if ci >= 1:
                p.wait(v, f"lc_app{ci - 1}")
            p.mark(f"lc_cp{cid}", v.tensor_copy(out=gl[:], in_=x1ic[:]), "v")

        @p.sec("t")
        def _(t, ci=ci, cid=cid):
            p.wait(t, f"lc_sq{cid}")
            if ci == 0:
                gc.wait_all(t)
            if ci >= 1:
                p.wait(t, f"lc_app{ci - 1}")
            p.wait(t, f"lc_cp{cid}")
            for k in range(KD):
                t.matmul(p.ps_t[0:1, 0:TC], onesc16[:], gl[:, k * TC:(k + 1) * TC],
                         start=(k == 0), stop=(k == KD - 1))
            for k in range(KD):
                ins = t.matmul(p.ps_t[0:1, 512:512 + TC], onesc16[:],
                               sqc[:, k * TC:(k + 1) * TC],
                               start=(k == 0), stop=(k == KD - 1))
            p.mark(f"lc_sum{cid}", ins, "t")

        @p.sec("v")
        def _(v, ci=ci, cid=cid):
            p.wait(v, f"lc_sum{cid}")
            v.tensor_scalar(out=row[:, 0:TC], in0=p.ps_t[0:1, 0:TC],
                            scalar1=1.0 / D, scalar2=None, op0=ALU.mult)
            v.tensor_scalar(out=row[:, TC:2 * TC], in0=p.ps_t[0:1, 512:512 + TC],
                            scalar1=1.0 / D, scalar2=None, op0=ALU.mult)
            v.drain()
            v.tensor_tensor(out=row[:, 2 * TC:3 * TC], in0=row[:, 0:TC],
                            in1=row[:, 0:TC], op=ALU.mult)
            v.drain()
            v.tensor_sub(row[:, TC:2 * TC], row[:, TC:2 * TC], row[:, 2 * TC:3 * TC])
            v.drain()
            p.mark(f"lc_var{cid}", v.tensor_scalar(
                out=row[:, TC:2 * TC], in0=row[:, TC:2 * TC], scalar1=1.0,
                scalar2=EPS, op0=ALU.mult, op1=ALU.add), "v")

        @p.sec("s")
        def _(s, cid=cid):
            p.wait(s, f"lc_var{cid}")
            s.activation(out=row[:, 3 * TC:4 * TC], in_=row[:, TC:2 * TC], func=AF.Ln)
            s.drain()
            p.mark(f"lc_rsq{cid}", s.activation(
                out=row[:, 3 * TC:4 * TC], in_=row[:, 3 * TC:4 * TC],
                func=AF.Exp, scale=-0.5), "s")

        @p.sec("t")
        def _(t, cid=cid):
            p.wait(t, f"lc_rsq{cid}")
            t.matmul(p.ps_a[:, 0:TC], p.ones_sb[:], row[:, 0:TC],
                     start=True, stop=True)
            p.mark(f"lc_bc{cid}", t.matmul(
                p.ps_b[:, 0:TC], p.ones_sb[:], row[:, 3 * TC:4 * TC],
                start=True, stop=True), "t")

        @p.sec("v")
        def _(v, ci=ci, cid=cid):
            p.wait(v, f"lc_bc{cid}")
            for o in range(KD):
                sl = slice(o * TC, (o + 1) * TC)
                v.tensor_tensor(out=x1ic[:, sl], in0=x1ic[:, sl],
                                in1=p.ps_a[:, 0:TC], op=ALU.subtract)
            v.drain()
            for o in range(KD):
                sl = slice(o * TC, (o + 1) * TC)
                v.tensor_tensor(out=x1ic[:, sl], in0=x1ic[:, sl],
                                in1=p.ps_b[:, 0:TC], op=ALU.mult)
            v.drain()
            for o in range(KD):
                sl = slice(o * TC, (o + 1) * TC)
                v.tensor_scalar(
                    out=h16[:, o * T + ci * TC:o * T + (ci + 1) * TC],
                    in0=x1ic[:, sl], scalar1=g2v[:, o:o + 1],
                    scalar2=bb2[:, o:o + 1], op0=ALU.mult, op1=ALU.add)
            p.mark(f"lc_app{ci}", v.drain(), "v")

    for lv in range(LEV):
        gw = p.group(f"lc_w{lv}")

        @p.sec("g")
        def _(g, lv=lv, gw=gw):
            if lv >= 1:
                p.wait(g, f"lc_lvl{lv - 1}")
            for k in range(KD):
                gw.issue(g, w1[:, k * HID:(k + 1) * HID],
                         p.cw1_h[lv, k * P:(k + 1) * P, :])
            for k in range(HID // P):
                gw.issue(g, w2[:, k * D:(k + 1) * D],
                         p.cw2_h[lv, k * P:(k + 1) * P, :])

        gcb = p.group(f"lc_cb{lv}")

        @p.sec("q")
        def _(q, lv=lv, gcb=gcb):
            gcb.issue(q, b1v[:], p.cb1_h[lv])
            gcb.issue(q, b2v[:], p.cb2_h[lv])

        for ci in range(NCH):
            cid = f"cms{lv}_{ci}"

            for o in range(HID // P):
                @p.sec("t")
                def _(t, lv=lv, ci=ci, cid=cid, gw=gw, o=o):
                    if o == 0 and ci == 0:
                        gw.wait_all(t)
                        if lv == 0:
                            p.wait(t, f"lc_app{NCH - 1}")
                    ps = [p.ps_a, p.ps_b][o % 2]
                    if o >= 2:
                        p.wait(t, f"lc_ge{cid}_{o - 2}")
                    elif ci >= 1:
                        p.wait(t, f"lc_ge{f'cms{lv}_{ci - 1}'}_{HID // P - 2 + o}")
                    elif lv >= 1 and ci == 0:
                        p.wait(t, f"lc_ge{f'cms{lv - 1}_{NCH - 1}'}_{HID // P - 2 + o}")
                    for k in range(KD):
                        ins = t.matmul(
                            ps[:, 0:TC], w1[:, k * HID + o * P:k * HID + (o + 1) * P],
                            h16[:, k * T + ci * TC:k * T + (ci + 1) * TC],
                            start=(k == 0), stop=(k == KD - 1))
                    p.mark(f"lc_up{cid}_{o}", ins, "t")

                @p.sec("s")
                def _(s, cid=cid, o=o, lv=lv, gcb=gcb):
                    p.wait(s, f"lc_up{cid}_{o}")
                    if o == 0:
                        gcb.wait_all(s)
                    ps = [p.ps_a, p.ps_b][o % 2]
                    p.mark(f"lc_ge{cid}_{o}", s.activation(
                        out=hid16[:, o * TC:(o + 1) * TC], in_=ps[:, 0:TC],
                        func=AF.Gelu_apprx_tanh, bias=b1v[:, o:o + 1]), "s")

            @p.sec("t")
            def _(t, lv=lv, ci=ci, cid=cid):
                p.wait(t, f"lc_ge{cid}_{HID // P - 1}")
                if ci >= 1:
                    p.wait(t, f"lc_dn{f'cms{lv}_{ci - 1}'}")
                for o in range(KD):
                    for k in range(HID // P):
                        ins = t.matmul(p.ps_y[:, o * TC:(o + 1) * TC],
                                       w2[:, k * D + o * P:k * D + (o + 1) * P],
                                       hid16[:, k * TC:(k + 1) * TC],
                                       start=(k == 0), stop=(k == HID // P - 1))
                p.mark(f"lc_dnmm{cid}", ins, "t")

            @p.sec("s")
            def _(s, lv=lv, ci=ci, cid=cid):
                p.wait(s, f"lc_dnmm{cid}")
                for o in range(KD):
                    s.activation(out=h16[:, o * T + ci * TC:o * T + (ci + 1) * TC],
                                 in_=p.ps_y[:, o * TC:(o + 1) * TC],
                                 func=AF.Identity, bias=b2v[:, o:o + 1])
                p.mark(f"lc_dn{cid}", s.drain(), "s")

        @p.sec("t")
        def _(t, lv=lv):
            p.mark(f"lc_lvl{lv}", t.drain(), "t")



def emit_l1d(p):
    h16 = p.h16_sb
    x1ic = p.sb("ld_x1i", [P, KD * TC], F32)
    gl = p.sb("ld_gl", [P, KD * TC], BF16)
    x1t = p.sb("ld_x1t", [P, 2 * D], F32)
    for ci in range(NCH):
        cid = f"fin{ci}"

        @p.sec("q")
        def _(q, ci=ci):
            st1 = p.stream("lc_fx1i", 2)
            st2 = p.stream("lc_fg1", 2)
            st3 = p.stream("lc_fx1", 2)
            if ci >= 1:
                p.wait(q, f"lc_x2{ci - 1}")
            st1.issue(q, x1ic.ap().rearrange("pp (o t) -> pp o t", o=KD),
                      p.x1i_h.rearrange("pp (o t) -> pp o t", t=T)[
                          :, :, ci * TC:(ci + 1) * TC])
            st2.issue(q, gl.ap().rearrange("pp (o t) -> pp o t", o=KD),
                      p.g1_h.rearrange("pp (o t) -> pp o t", t=T)[
                          :, :, ci * TC:(ci + 1) * TC])
            st3.issue(q, x1t.ap().rearrange("pp (a dd) -> pp a dd", a=2),
                      p.x1_h[ci * TC:(ci + 1) * TC, :].rearrange(
                          "(a pp) dd -> pp a dd", pp=P))

        @p.sec("v")
        def _(v, ci=ci, cid=cid):
            p.stream("lc_fx1i", 2).wait(v, ci)
            p.stream("lc_fg1", 2).wait(v, ci)
            if ci == 0:
                p.wait(v, f"lc_dn{f'cms{LEV - 1}_{NCH - 1}'}")
            for o in range(KD):
                sl = slice(o * TC, (o + 1) * TC)
                v.tensor_tensor(out=x1ic[:, sl], in0=x1ic[:, sl],
                                in1=h16[:, o * T + ci * TC:o * T + (ci + 1) * TC],
                                op=ALU.add)
            v.drain()
            v.tensor_tensor(out=x1ic[:], in0=x1ic[:], in1=gl[:], op=ALU.mult)
            p.mark(f"lc_sg{cid}", v.drain(), "v")

        @p.sec("t")
        def _(t, ci=ci, cid=cid):
            p.wait(t, f"lc_sg{cid}")
            if ci >= 1:
                p.wait(t, f"lc_x2v{ci - 1}")
            for h in range(2):
                for o in range(KD):
                    idx = h * KD + o
                    ins = t.transpose(p.ps_y[:, idx * P:(idx + 1) * P],
                                      x1ic[:, o * TC + h * P:o * TC + (h + 1) * P],
                                      p.ident_sb[:])
                    if idx % 8 == 7:
                        p.mark(f"lc_ftr{cid}_{idx // 8}", ins, "t")

        @p.sec("v")
        def _(v, ci=ci, cid=cid):
            p.stream("lc_fx1", 2).wait(v, ci)
            for h in range(2):
                p.wait(v, f"lc_ftr{cid}_{h}")
            for h in range(2):
                for o in range(KD):
                    idx = h * KD + o
                    v.tensor_tensor(out=x1t[:, h * D + o * P:h * D + (o + 1) * P],
                                    in0=p.ps_y[:, idx * P:(idx + 1) * P],
                                    in1=x1t[:, h * D + o * P:h * D + (o + 1) * P],
                                    op=ALU.add)
            p.mark(f"lc_x2v{ci}", v.drain(), "v")

        @p.sec("q")
        def _(q, ci=ci):
            p.wait(q, f"lc_x2v{ci}")
            st_ = p.stream("lc_out", 2)
            st_.issue(q, p.out_h[ci * TC:(ci + 1) * TC, :].rearrange(
                "(a pp) dd -> pp a dd", pp=P),
                x1t.ap().rearrange("pp (a dd) -> pp a dd", a=2))
            p.cnt["q"] += 1
            q.sem_inc(p.sem["q"], 1)
            p.ev.setdefault(f"lc_x2{ci}", []).append(("q", p.cnt["q"]))




def _prep_weights(inputs):
    f = np.ascontiguousarray

    def T_(a):
        return f(np.transpose(a).astype(np.float32))

    d = {}
    d["w_inT"] = T_(inputs["L0_attn_W_in"])
    cw_ = np.stack([inputs["L0_attn_conv_w"][:, 0, :], inputs["L0_attn_conv_w_b"][:, 0, :]]).astype(np.float32)
    d["conv_w"] = f(np.transpose(cw_.reshape(2, KDI, P, 4), (0, 2, 1, 3)).reshape(2, P, KDI * 4))
    cb_ = np.stack([inputs["L0_attn_conv_b"], inputs["L0_attn_conv_b_b"]]).astype(np.float32)
    d["conv_b"] = f(np.transpose(cb_.reshape(2, KDI, P), (0, 2, 1)))
    wx_ = np.stack([T_(inputs["L0_attn_Wx"]), T_(inputs["L0_attn_Wx_b"])])
    wxp = np.zeros((2, DI, 112), np.float32)
    wxp[:, :, 0:80] = wx_[:, :, 0:80]
    wxp[:, :, 96:112] = wx_[:, :, 80:96]
    d["wxT"] = f(wxp)
    d["wdtT"] = f(np.stack([T_(inputs["L0_attn_Wdt"]), T_(inputs["L0_attn_Wdt_b"])]))
    bd_ = np.stack([inputs["L0_attn_bdt"], inputs["L0_attn_bdt_b"]]).astype(np.float32)
    d["bdt"] = f(np.transpose(bd_.reshape(2, KDI, P), (0, 2, 1)))
    al = []
    for a in (inputs["L0_attn_A_log"], inputs["L0_attn_A_b_log"]):
        al.append(np.transpose(np.asarray(a, np.float32).reshape(KDI, P, NST),
                               (1, 0, 2)).reshape(P, KDI * NST))
    d["a_log"] = f(np.stack(al))
    ds_ = np.stack([inputs["L0_attn_D"], inputs["L0_attn_D_b"]]).astype(np.float32)
    d["d_skip"] = f(np.transpose(ds_.reshape(2, KDI, P), (0, 2, 1)))
    d["w_outT"] = T_(inputs["L0_attn_W_out"])
    d["wr_T"] = T_(inputs["L0_ffn_Wr"])
    d["br"] = f(np.asarray(inputs["L0_ffn_br"], np.float32)[None, :])
    d["biases_n"] = f(np.asarray(inputs["L0_ffn_biases_N"], np.float32)[None, :])
    d["wl1"] = f(np.asarray(inputs["L0_ffn_Wl1"], np.float32))
    d["bl1"] = f(np.transpose(np.asarray(inputs["L0_ffn_bl1"], np.float32).reshape(NEXP, 32, P), (0, 2, 1)))
    d["wl2"] = f(np.asarray(inputs["L0_ffn_Wl2"], np.float32))
    d["bl2"] = f(np.transpose(np.asarray(inputs["L0_ffn_bl2"], np.float32).reshape(NEXP, 8, P), (0, 2, 1)))
    d["bl2r"] = f(np.asarray(inputs["L0_ffn_bl2"], np.float32))
    d["ln1_g"] = f(np.asarray(inputs["L1_attn_ln1_g"], np.float32).reshape(KD, P).T.copy())
    d["ln1_b"] = f(np.asarray(inputs["L1_attn_ln1_b"], np.float32).reshape(KD, P).T.copy())
    d["wq"] = f(np.asarray(inputs["L1_attn_wq"], np.float32))
    d["wk"] = f(np.asarray(inputs["L1_attn_wk"], np.float32))
    d["wv"] = f(np.asarray(inputs["L1_attn_wv"], np.float32))
    wo = np.asarray(inputs["L1_attn_wo"], np.float32).reshape(HEADS, HD, D)
    kt = np.empty((KD, P, D), np.float32)
    for hp in range(KD):
        kt[hp, 0:HD] = wo[2 * hp]
        kt[hp, HD:P] = wo[2 * hp + 1]
    d["wo_kt"] = f(kt.reshape(D, D))
    d["ln2_g"] = f(np.asarray(inputs["L1_attn_ln2_g"], np.float32).reshape(KD, P).T.copy())
    d["ln2_b"] = f(np.asarray(inputs["L1_attn_ln2_b"], np.float32).reshape(KD, P).T.copy())
    d["cms_w1"] = f(np.asarray(inputs["L1_attn_cms_w1"], np.float32))
    d["cms_b1"] = f(np.transpose(np.asarray(inputs["L1_attn_cms_b1"], np.float32).reshape(LEV, HID // P, P), (0, 2, 1)))
    d["cms_w2"] = f(np.asarray(inputs["L1_attn_cms_w2"], np.float32))
    d["cms_b2"] = f(np.transpose(np.asarray(inputs["L1_attn_cms_b2"], np.float32).reshape(LEV, KD, P), (0, 2, 1)))
    d["ffn_wT"] = T_(inputs["L1_ffn_W"])
    d["ffn_b"] = f(np.asarray(inputs["L1_ffn_b"], np.float32).reshape(KD, P).T.copy())
    d["ffn_vr"] = f(np.asarray(inputs["L1_ffn_V_r"], np.float32))
    d["ffn_cT"] = T_(inputs["L1_ffn_C"])
    return d


def run(inputs, dbg_spec=(), upto="full", cores=8):
    w = _prep_weights(inputs)
    hidden = np.asarray(inputs["L0_attn_hidden"], np.float32)
    prog = build(dbg_spec, upto=upto)
    in_maps = []
    for b in range(cores):
        m = dict(w)
        m["x0"] = np.ascontiguousarray(hidden[b])
        in_maps.append(m)
    res = run_bass_kernel_spmd(prog.nc, in_maps, core_ids=list(range(cores)))
    return res


def kernel(**inputs):
    res = run(inputs, upto="full")
    out = np.stack([np.asarray(r["out"]) for r in res.results])
    return out.astype(np.float32)


# version 02 (final)
# speedup vs baseline: 1.0208x; 1.0208x over previous
import contextlib
import numpy as np
import concourse.bass as bass
import concourse.mybir as mybir
from concourse.bass_utils import run_bass_kernel_spmd

F32 = mybir.dt.float32
BF16 = mybir.dt.bfloat16
AF = mybir.ActivationFunctionType
ALU = mybir.AluOpType
AX = mybir.AxisListType.X

P = 128
T = 2048
D = 1024
DI = 2048
NST = 16
DTR = 64
TC = 256
NCH = T // TC
NT = T // P
KD = D // P
KDI = DI // P
NEXP = 8
HMOE = 2048
TB = 256
NTB = T // TB
HEADS = 16
HD = 64
ACH = 128
NACH = T // ACH
LEV = 3
HID = 4096
EPS = 1e-5
BIG = 1e30


class DmaStream:

    def __init__(self, prog, name, depth):
        self.p = prog
        self.name = name
        self.d = depth
        self.sems = [prog.ctx.enter_context(prog.nc.semaphore(f"dma_{name}{i}"))
                     for i in range(depth)]
        self.n = 0

    def issue(self, eng, out, in_, **kw):
        i = self.n
        slot = i % self.d
        if i >= self.d:
            eng.wait_ge(self.sems[slot], 16 * (i // self.d))
        eng.dma_start(out=out, in_=in_, **kw).then_inc(self.sems[slot], 16)
        self.n += 1
        return i

    def wait(self, eng, i=None):
        if i is None:
            i = self.n - 1
        if i < 0:
            return
        eng.wait_ge(self.sems[i % self.d], 16 * (i // self.d + 1))

    def wait_all(self, eng):
        for s in range(min(self.d, self.n)):
            last = (self.n - 1 - s)
            eng.wait_ge(self.sems[last % self.d], 16 * (last // self.d + 1))


class DmaGroup:

    def __init__(self, prog, name):
        self.sem = prog.ctx.enter_context(prog.nc.semaphore(f"dmg_{name}"))
        self.n = 0
        self.closed = False

    def issue(self, eng, out, in_, **kw):
        assert not self.closed
        eng.dma_start(out=out, in_=in_, **kw).then_inc(self.sem, 16)
        self.n += 1

    def wait_all(self, eng):
        self.closed = True
        eng.wait_ge(self.sem, 16 * self.n)


class Prog:
    ENGNAME = {"t": "tensor", "v": "vector", "s": "scalar", "g": "gpsimd", "q": "sync"}

    def __init__(self, dbg_spec=()):
        self.nc = bass.Bass("TRN2", target_bir_lowering=False,
                            detect_race_conditions=False)
        self.ctx = contextlib.ExitStack()
        nc = self.nc
        self.sem = {}
        for e in ("t", "v", "s", "g", "q"):
            self.sem[e] = self.ctx.enter_context(nc.semaphore("sem_" + e))
        self.cnt = {k: 0 for k in self.sem}
        self.ev = {}
        self.streams = {}
        self.phase = None
        self.dbg = {}
        for name, shape, dt in dbg_spec:
            self.dbg[name] = nc.dram_tensor("dbg_" + name, list(shape), dt,
                                            kind="ExternalOutput").ap()

    def stream(self, name, depth=2):
        if name not in self.streams:
            self.streams[name] = DmaStream(self, name, depth)
        return self.streams[name]

    def group(self, name):
        if name not in self.streams:
            self.streams[name] = DmaGroup(self, name)
        return self.streams[name]

    def mark(self, name, instr, sem, inc=1):
        instr.then_inc(self.sem[sem], inc)
        self.cnt[sem] += inc
        self.ev.setdefault(name, []).append((sem, self.cnt[sem]))

    def wait(self, eng, name, idx=-1):
        sname, val = self.ev[name][idx]
        eng.wait_ge(self.sem[sname], val)

    def waitn(self, eng, name):
        if name in self.ev:
            self.wait(eng, name, -1)

    def sec(self, e):
        return getattr(self.block, self.ENGNAME[e])

    def barrier(self):
        snap = {}
        for e in ("t", "v", "s", "g", "q"):
            @self.sec(e)
            def _(eng, e=e):
                if e == "q":
                    eng.sem_inc(self.sem[e], 1)
                else:
                    eng.drain().then_inc(self.sem[e], 1)
            self.cnt[e] += 1
            snap[e] = self.cnt[e]
        for e in ("t", "v", "s", "g", "q"):
            @self.sec(e)
            def _(eng):
                for i2, e2 in enumerate(("t", "v", "s", "g", "q")):
                    if i2:
                        eng.nop()
                    eng.wait_ge(self.sem[e2], snap[e2])

    def sb(self, name, shape, dt, persistent=False):
        tgt = self.ctx if (persistent or self.phase is None) else self.phase
        return tgt.enter_context(self.nc.sbuf_tensor(name, shape, dt))

    def open_phase(self):
        assert self.phase is None
        self.phase = contextlib.ExitStack()

    def close_phase(self):
        self.phase.close()
        self.phase = None

    def pst(self, name, shape):
        return self.ctx.enter_context(self.nc.psum_tensor(name, shape, F32))

    def din(self, name, shape, dt=F32):
        return self.nc.dram_tensor(name, list(shape), dt, kind="ExternalInput").ap()

    def scrh(self, name, shape, dt):
        return self.nc.dram_tensor(name, list(shape), dt, kind="Internal").ap()

    def ln_stats_apply(self, i, xt, nt, st, tag):
        p = self

        @p.sec("v")
        def _(v):
            v.reduce_sum(out=st[:, 0:1], in_=xt, axis=AX)
            v.tensor_mul(nt, xt, xt)
            v.reduce_sum(out=st[:, 1:2], in_=nt, axis=AX)
            v.drain()
            v.tensor_scalar(out=st[:, 0:2], in0=st[:, 0:2], scalar1=1.0 / D,
                            scalar2=None, op0=ALU.mult)
            v.drain()
            v.tensor_mul(st[:, 2:3], st[:, 0:1], st[:, 0:1])
            v.drain()
            v.tensor_sub(st[:, 1:2], st[:, 1:2], st[:, 2:3])
            v.drain()
            p.mark(f"{tag}_var{i}", v.tensor_scalar(
                out=st[:, 1:2], in0=st[:, 1:2], scalar1=1.0, scalar2=EPS,
                op0=ALU.mult, op1=ALU.add), "v")

        @p.sec("s")
        def _(s):
            p.wait(s, f"{tag}_var{i}")
            s.activation(out=st[:, 5:6], in_=st[:, 1:2], func=AF.Ln)
            p.mark(f"{tag}_rsq{i}", s.activation(
                out=st[:, 4:5], in_=st[:, 5:6], func=AF.Exp, scale=-0.5), "s")

        @p.sec("v")
        def _(v):
            p.wait(v, f"{tag}_rsq{i}")
            p.mark(f"{tag}_xn{i}", v.scalar_tensor_tensor(
                out=nt, in0=xt, scalar=st[:, 0:1],
                in1=st[:, 4:5].broadcast_to([P, D]),
                op0=ALU.subtract, op1=ALU.mult), "v")


def build(dbg_spec=(), upto="p1"):
    p = Prog(dbg_spec)
    nc = p.nc

    p.x0_h = p.din("x0", (T, D))
    p.w_inT_h = p.din("w_inT", (D, 2 * DI))
    p.conv_w_h = p.din("conv_w", (2, P, KDI * 4))
    p.conv_b_h = p.din("conv_b", (2, P, KDI))
    p.wxT_h = p.din("wxT", (2, DI, 112))
    p.wdtT_h = p.din("wdtT", (2, DTR, DI))
    p.bdt_h = p.din("bdt", (2, P, KDI))
    p.a_log_h = p.din("a_log", (2, P, KDI * NST))
    p.d_skip_h = p.din("d_skip", (2, P, KDI))
    p.w_outT_h = p.din("w_outT", (DI, D))
    p.wr_T_h = p.din("wr_T", (D, NEXP))
    p.br_h = p.din("br", (1, NEXP))
    p.biases_h = p.din("biases_n", (1, NEXP))
    p.wl1_h = p.din("wl1", (NEXP, D, 2 * HMOE))
    p.bl1_h = p.din("bl1", (NEXP, P, 32))
    p.wl2_h = p.din("wl2", (NEXP, HMOE, D))
    p.bl2_h = p.din("bl2", (NEXP, P, 8))
    p.bl2r_h = p.din("bl2r", (NEXP, D))
    p.ln1_g_h = p.din("ln1_g", (P, KD))
    p.ln1_b_h = p.din("ln1_b", (P, KD))
    p.wq_h = p.din("wq", (D, D))
    p.wk_h = p.din("wk", (D, D))
    p.wv_h = p.din("wv", (D, D))
    p.wo_h = p.din("wo_kt", (D, D))
    p.ln2_g_h = p.din("ln2_g", (P, KD))
    p.ln2_b_h = p.din("ln2_b", (P, KD))
    p.cw1_h = p.din("cms_w1", (LEV, D, HID))
    p.cb1_h = p.din("cms_b1", (LEV, P, HID // P))
    p.cw2_h = p.din("cms_w2", (LEV, HID, D))
    p.cb2_h = p.din("cms_b2", (LEV, P, KD))
    p.fw_h = p.din("ffn_wT", (D, D))
    p.fb_h = p.din("ffn_b", (P, KD))
    p.fvr_h = p.din("ffn_vr", (D, 32))
    p.fcT_h = p.din("ffn_cT", (32, D))
    p.out_h = nc.dram_tensor("out", [T, D], F32, kind="ExternalOutput").ap()

    p.xz_h = p.scrh("xz_s", (2 * DI, T + 8), BF16)
    p.xzr_h = p.scrh("xzr_s", (2 * DI, T + 8), BF16)
    p.outf_h = p.scrh("outf_s", (DI, T), BF16)
    p.outb_h = p.scrh("outb_s", (DI, T), BF16)
    p.ymoe_h = p.scrh("ymoe_s", (P, NEXP * T), F32)
    p.x1_h = p.scrh("x1_s", (T, D), F32)
    p.xn1fm_h = p.scrh("xn1_fm_s", (D, T), BF16)
    p.xn1tm_h = p.scrh("xn1_tm_s", (T, D), F32)
    p.g1_h = p.scrh("g1_s", (P, KD * T), BF16)
    p.x1i_h = p.scrh("x1i_s", (P, KD * T), F32)
    p.ln2fm_h = p.scrh("ln2_fm_s", (D, T), BF16)
    p.val_h = p.scrh("val_s", (P, P), F32)
    p.sb_h = p.scrh("sb_s", (P, KD * T), F32)

    p.ident_c = nc.inline_tensor(np.eye(P, dtype=np.float32), "ident").ap()
    p.ones_c = nc.inline_tensor(np.ones((1, P), dtype=np.float32), "ones_row").ap()
    import ml_dtypes as _mld
    p.ones16_c = nc.inline_tensor(np.ones((1, P), dtype=_mld.bfloat16), "ones16_row").ap()
    p.triuT_c = nc.inline_tensor(
        np.triu(np.ones((ACH, ACH), dtype=np.float32)), "triuT").ap()
    oh = np.zeros((NST, NST * P), np.float32)
    for n in range(NST):
        oh[n, n * P:(n + 1) * P] = 1.0
    p.onehot_c = nc.inline_tensor(oh, "onehot16").ap()
    p.onehotb_c = nc.inline_tensor(oh.astype(_mld.bfloat16), "onehot16b").ap()
    p.onesc16_c = nc.inline_tensor(np.ones((P, 1), dtype=_mld.bfloat16), "onesc16").ap()
    p.identb_c = nc.inline_tensor(np.eye(P, dtype=_mld.bfloat16), "identb").ap()

    p.val_fm = p.sb("val_fm", [P, P], F32, persistent=True)
    p.ident_sb = p.sb("ident_sb", [P, P], F32, persistent=True)
    p.ones_sb = p.sb("ones_sb", [1, P], F32, persistent=True)
    p.ones16_sb = p.sb("ones16_sb", [1, P], BF16, persistent=True)
    p.onehotb_sb = p.sb("onehotb_sb", [NST, NST * P], BF16, persistent=True)
    p.triuT_sb = p.sb("triuT_sb", [ACH, ACH], F32, persistent=True)
    p.identb_sb = p.sb("identb_sb", [P, P], BF16, persistent=True)
    p.xn_scope = contextlib.ExitStack()
    p.xn_fm = p.xn_scope.enter_context(nc.sbuf_tensor("xn_fm", [P, KD * T], BF16))

    p.ps_a = p.pst("ps_a", [P, 512])
    p.ps_b = p.pst("ps_b", [P, 512])
    p.ps_t = p.pst("ps_t", [P, 1024])
    p.ps_y = p.pst("ps_y", [P, 2048])

    blk = nc.Block()
    p.block = blk.__enter__()

    p.open_phase()
    emit_p0(p)
    p.barrier()
    p.close_phase()
    if upto != "p0nr":
        p.open_phase()
        emit_router(p, stage=float(upto[1:]) if upto.startswith("r") else 9)
        p.barrier()
        p.close_phase()
    if upto not in ("p0", "p0nr") and not upto.startswith("r"):
        p.open_phase()
        emit_p1(p)
        p.barrier()
        p.close_phase()
    if upto in ("p2a", "p2", "p3", "l1a", "l1b", "l1c", "full"):
        p.open_phase()
        emit_p2a_moe(p)
        p.barrier()
        p.close_phase()
        if upto in ("p2", "p3", "l1a", "l1b", "l1c", "full"):
            p.xn_scope.close()
            p.open_phase()
            emit_p2b(p)
            p.close_phase()
        if upto in ("l1a", "l1b", "l1c", "full"):
            p.l1_scope = contextlib.ExitStack()
            p.xn1_fm_sb = p.l1_scope.enter_context(
                p.nc.sbuf_tensor("xn1_fm", [P, KD * T], BF16))
            p.ln1_fm_sb = p.l1_scope.enter_context(
                p.nc.sbuf_tensor("ln1_fm", [P, KD * T], BF16))
            p.open_phase()
            emit_l1a(p)
            p.barrier()
            p.close_phase()
            if upto != "l1a":
                p.open_phase()
                emit_l1b(p)
                p.barrier()
                p.close_phase()
            p.l1_scope.close()
            if upto in ("l1c", "full"):
                p.l1h_scope = contextlib.ExitStack()
                p.h16_sb = p.l1h_scope.enter_context(
                    p.nc.sbuf_tensor("h16", [P, KD * T], BF16))
                p.open_phase()
                emit_l1c(p)
                p.barrier()
                p.close_phase()
                p.open_phase()
                emit_l1d(p)
                p.barrier()
                p.close_phase()
                p.l1h_scope.close()
        if "x1" in p.dbg:
            @p.sec("q")
            def _(q):
                for st_ in p.streams.values():
                    st_.wait_all(q)
                p.group("dbg_x1").issue(q, p.dbg["x1"], p.x1_h)
        if "ymoe" in p.dbg:
            @p.sec("q")
            def _(q):
                for st_ in p.streams.values():
                    st_.wait_all(q)
                p.group("dbg_ymoe").issue(q, p.dbg["ymoe"], p.ymoe_h)

    if "xz" in p.dbg:
        @p.sec("q")
        def _(q):
            if "xz_st" in p.streams:
                p.streams["xz_st"].wait_all(q)
            p.group("dbg_xz").issue(q, p.dbg["xz"], p.xz_h[:, :])

    @p.sec("q")
    def _(q):
        for st_ in p.streams.values():
            st_.wait_all(q)

    blk.__exit__(None, None, None)
    p.ctx.close()
    return p


def emit_p0(p):
    x_t = p.sb("p0_x", [P, 2 * D], F32)
    xn_t = p.sb("p0_xn", [P, 2 * D], F32)
    st = p.sb("p0_stat", [P, 32], F32)

    gconst = p.group("const")

    @p.sec("q")
    def _(q):
        for o_, i_ in ((p.ident_sb[:], p.ident_c), (p.ones_sb[:], p.ones_c),
                       (p.ones16_sb[:], p.ones16_c), (p.onehotb_sb[:], p.onehotb_c),
                       (p.triuT_sb[:], p.triuT_c), (p.identb_sb[:], p.identb_c)):
            gconst.issue(q, o_, i_)

    for i in range(NT):
        par = i % 2
        xt = x_t[:, par * D:(par + 1) * D]
        nt = xn_t[:, par * D:(par + 1) * D]
        sti = st[:, par * 16:(par + 1) * 16]

        @p.sec("q")
        def _(q, i=i, xt=xt):
            if i >= 2:
                p.wait(q, f"p0_xn{i - 2}")
            p.stream("x0ld").issue(q, xt, p.x0_h[i * P:(i + 1) * P, :])

        @p.sec("v")
        def _(v, i=i):
            p.stream("x0ld").wait(v, i)
            if i >= 2:
                p.wait(v, f"p0_tr{i - 2}_1")
        p.ln_stats_apply(i, xt, nt, sti, "p0")

        if "xn_tm" in p.dbg:
            @p.sec("q")
            def _(q, i=i, nt=nt):
                p.wait(q, f"p0_xn{i}")
                p.stream("dbg_xntm").issue(q, p.dbg["xn_tm"][i * P:(i + 1) * P, :], nt)

        @p.sec("t")
        def _(t, i=i, nt=nt):
            if i == 0:
                gconst.wait_all(t)
            p.wait(t, f"p0_xn{i}")
            if "xn_tm" in p.dbg:
                p.stream("dbg_xntm").wait(t, i)
            if i >= 1:
                p.wait(t, f"p0_evB{i - 1}")
            for k in range(KD):
                ins = t.transpose(p.ps_t[:, k * P:(k + 1) * P],
                                  nt[:, k * P:(k + 1) * P], p.ident_sb[:])
                if k % 4 == 3:
                    p.mark(f"p0_tr{i}_{k // 4}", ins, "t")

        @p.sec("s")
        def _(s, i=i):
            for h2 in range(2):
                p.wait(s, f"p0_tr{i}_{h2}")
                for k in range(4):
                    kk = h2 * 4 + k
                    ins = s.activation(
                        out=p.xn_fm[:, kk * T + i * P: kk * T + (i + 1) * P],
                        in_=p.ps_t[:, kk * P:(kk + 1) * P],
                        func=AF.Copy)
                p.mark(f"p0_evA{i}" if h2 == 0 else f"p0_evB{i}", ins, "s")

    if "xn_fm" in p.dbg:
        @p.sec("q")
        def _(q):
            p.wait(q, f"p0_evB{NT - 1}")
            gd = p.group("dbg_xn")
            for k in range(KD):
                gd.issue(q, p.dbg["xn_fm"][k * P:(k + 1) * P, :],
                         p.xn_fm[:, k * T:(k + 1) * T])


def emit_router(p, stage=9):
    wr_sb = p.sb("r_wr", [P, KD * NEXP], BF16)
    brr = p.sb("r_br", [1, 2 * NEXP], F32)
    brtm = p.sb("r_brtm", [P, 2 * NEXP], F32)
    sctm = p.sb("r_sctm", [P, NT * NEXP], F32)
    scbtm = p.sb("r_scbtm", [P, NT * NEXP], F32)
    aux = p.sb("r_aux", [P, 8 * NT], F32)
    vtm = p.sb("r_vtm", [P, NT * NEXP], F32)

    gwr = p.group("r_wr")

    @p.sec("g")
    def _(g):
        for k in range(KD):
            gwr.issue(g, wr_sb[:, k * NEXP:(k + 1) * NEXP],
                      p.wr_T_h[k * P:(k + 1) * P, :])

    gbr = p.group("r_br")

    @p.sec("q")
    def _(q):
        gbr.issue(q, brr[:, 0:NEXP], p.br_h)
        gbr.issue(q, brr[:, NEXP:], p.biases_h)

    @p.sec("t")
    def _(t):
        gbr.wait_all(t)
        p.mark("r_brmm", t.matmul(p.ps_t[:, 0:2 * NEXP], p.ones_sb[:], brr[:],
                                  start=True, stop=True), "t")

    @p.sec("v")
    def _(v):
        p.wait(v, "r_brmm")
        v.tensor_copy(out=brtm[:, 0:NEXP], in_=p.ps_t[:, 0:NEXP])
        v.drain()
        p.mark("r_brtm", v.tensor_tensor(out=brtm[:, NEXP:], in0=brtm[:, 0:NEXP],
                                         in1=p.ps_t[:, NEXP:2 * NEXP], op=ALU.add), "v")

    if stage < 0.5:
        return

    NB = NT * NEXP
    brt2 = p.sb("r_brt2", [P, 2 * NB], F32)

    @p.sec("v")
    def _(v):
        p.wait(v, "r_brtm")
        for tb in range(NT):
            v.tensor_copy(out=brt2[:, tb * NEXP:(tb + 1) * NEXP], in_=brtm[:, 0:NEXP])
            v.tensor_copy(out=brt2[:, NB + tb * NEXP:NB + (tb + 1) * NEXP],
                          in_=brtm[:, NEXP:])
        p.mark("r_brt2", v.drain(), "v")

    @p.sec("t")
    def _(t):
        gwr.wait_all(t)
        for tb in range(NT):
            for k in range(KD):
                ins = t.matmul(p.ps_a[:, tb * NEXP:(tb + 1) * NEXP],
                               p.xn_fm[:, k * T + tb * P:k * T + (tb + 1) * P],
                               wr_sb[:, k * NEXP:(k + 1) * NEXP],
                               start=(k == 0), stop=(k == KD - 1))
        p.mark("r_mm", ins, "t")

    @p.sec("v")
    def _(v):
        p.wait(v, "r_brt2")
        p.wait(v, "r_mm")
        v.tensor_tensor(out=sctm[:, 0:NB], in0=p.ps_a[:, 0:NB],
                        in1=brt2[:, 0:NB], op=ALU.add)
        p.mark(f"r_ev{NT - 1}", v.tensor_tensor(
            out=scbtm[:, 0:NB], in0=p.ps_a[:, 0:NB],
            in1=brt2[:, NB:], op=ALU.add), "v")

    NB = NT * NEXP
    if stage < 2:
        return

    for tb in range(NT):
        sl = slice(tb * NEXP, (tb + 1) * NEXP)

        @p.sec("v")
        def _(v, tb=tb, sl=sl):
            if tb == 0:
                p.wait(v, f"r_ev{NT - 1}")
            sc = sctm[:, sl]
            scb = scbtm[:, sl]
            v3 = vtm[:, sl]
            m1 = aux[:, 0:1]
            m2 = aux[:, 1:2]
            mx = aux[:, 2:3]

            def b(col):
                return col.broadcast_to([P, NEXP])

            v.reduce_max(out=m1, in_=scb, axis=AX)
            v.drain()
            v.tensor_tensor(out=v3, in0=scb, in1=b(m1), op=ALU.is_ge)
            v.drain()
            v.scalar_tensor_tensor(out=v3, in0=v3, scalar=-BIG, in1=scb,
                                   op0=ALU.mult, op1=ALU.add)
            v.drain()
            v.reduce_max(out=m2, in_=v3, axis=AX)
            v.drain()
            v.tensor_tensor(out=v3, in0=scb, in1=b(m2), op=ALU.is_ge)
            v.drain()
            v.reduce_max(out=mx, in_=sc, axis=AX)
            v.drain()
            ins = v.tensor_tensor(out=sc, in0=sc, in1=b(mx), op=ALU.subtract)
            if tb == NT - 1:
                p.mark("r_sub", ins, "v")

    if stage < 3:
        return

    @p.sec("s")
    def _(s):
        p.wait(s, "r_sub")
        p.mark("r_exp", s.activation(out=scbtm[:, 0:NB], in_=sctm[:, 0:NB], func=AF.Exp), "s")

    for tb in range(NT):
        sl = slice(tb * NEXP, (tb + 1) * NEXP)

        @p.sec("v")
        def _(v, tb=tb, sl=sl):
            if tb == 0:
                p.wait(v, "r_exp")
            sc = scbtm[:, sl]
            v3 = vtm[:, sl]
            vs = aux[:, 3:4]

            def b(col):
                return col.broadcast_to([P, NEXP])

            v.tensor_tensor(out=v3, in0=v3, in1=sc, op=ALU.mult)
            v.drain()
            v.reduce_sum(out=vs, in_=v3, axis=AX)
            v.drain()
            v.reciprocal(out=vs, in_=vs)
            v.drain()
            ins = v.tensor_tensor(out=v3, in0=v3, in1=b(vs), op=ALU.mult)
            v.drain()
            if tb == NT - 1:
                p.mark("r_val", ins, "v")

    if stage < 4:
        return

    @p.sec("t")
    def _(t):
        p.wait(t, "r_val")
        p.mark("r_vtr", t.transpose(p.ps_t[:, 0:P], vtm[:, 0:P], p.ident_sb[:]), "t")

    @p.sec("v")
    def _(v):
        p.wait(v, "r_vtr")
        p.mark("r_vfm", v.tensor_copy(out=p.val_fm[:, 0:P], in_=p.ps_t[:, 0:P]), "v")

    @p.sec("q")
    def _(q):
        p.wait(q, "r_vfm")
        p.group("r_valst").issue(q, p.val_h, p.val_fm[:, 0:P])



    if "val_fm" in p.dbg:
        @p.sec("q")
        def _(q):
            p.wait(q, "r_vfm")
            p.group("dbg_val").issue(q, p.dbg["val_fm"], p.val_fm[:, 0:P])


def emit_p1(p):
    w_in_sb = p.sb("p1_win", [P, KD * 4096], BF16)
    xz_ev = p.sb("p1_ev", [P, 4 * 512], BF16)
    xz_rev = p.sb("p1_rev", [P, 4 * 512], BF16)
    TP = 512

    gwin = p.group("p1_win")

    @p.sec("g")
    def _(g):
        for k in range(KD):
            gwin.issue(g, w_in_sb[:, k * 4096:(k + 1) * 4096],
                       p.w_inT_h[k * P:(k + 1) * P, :])

    NO = 2 * DI // P
    sink = p.scrh("p1_sink", (P, 64), BF16)
    for c in range(T // TP):
        for o in range(NO):
            idx = c * NO + o
            ps = [p.ps_a, p.ps_b][o % 2]

            @p.sec("t")
            def _(t, c=c, o=o, idx=idx, ps=ps):
                if idx == 0:
                    gwin.wait_all(t)
                if idx >= 2:
                    p.wait(t, f"p1_ev{idx - 2}")
                for k in range(KD):
                    ins = t.matmul(ps[:, 0:TP],
                                   w_in_sb[:, k * 4096 + o * P:k * 4096 + (o + 1) * P],
                                   p.xn_fm[:, k * T + c * TP:k * T + (c + 1) * TP],
                                   start=(k == 0), stop=(k == KD - 1))
                p.mark(f"p1_mm{idx}", ins, "t")

            @p.sec("s")
            def _(s, idx=idx, ps=ps):
                p.wait(s, f"p1_mm{idx}")
                if idx >= 4:
                    p.stream("xz_st", 4).wait(s, idx - 4)
                    p.wait(s, f"p1_rev{idx - 4}")
                p.mark(f"p1_ev{idx}", s.activation(
                    out=xz_ev[:, (idx % 4) * TP:(idx % 4 + 1) * TP],
                    in_=ps[:, 0:TP], func=AF.Copy), "s")

            @p.sec("v")
            def _(v, idx=idx):
                p.wait(v, f"p1_ev{idx}")
                if idx >= 4:
                    p.stream("xzr_st", 4).wait(v, idx - 4)
                sl = slice((idx % 4) * TP, (idx % 4 + 1) * TP)
                p.mark(f"p1_rev{idx}", v.tensor_copy(
                    out=xz_rev[:, sl],
                    in_=xz_ev[:, (idx % 4 + 1) * TP - 1:(idx % 4) * TP - 1 if idx % 4 else None:-1]), "v")

            @p.sec("q")
            def _(q, c=c, o=o, idx=idx):
                p.wait(q, f"p1_ev{idx}")
                p.stream("xz_st", 4).issue(
                    q, p.xz_h[o * P:(o + 1) * P, c * TP:(c + 1) * TP],
                    xz_ev[:, (idx % 4) * TP:(idx % 4 + 1) * TP])
                p.wait(q, f"p1_rev{idx}")
                p.stream("xzr_st", 4).issue(
                    q, p.xzr_h[o * P:(o + 1) * P, T - (c + 1) * TP:T - c * TP],
                    xz_rev[:, (idx % 4) * TP:(idx % 4 + 1) * TP])



def emit_p2a_moe(p):
    wl1_sb = p.sb("moe_wl1", [P, KD * 4096], BF16)
    wl2_sb = p.sb("moe_wl2", [P, KDI * D], BF16)
    bl1_sb = p.sb("moe_bl1", [P, NEXP * 32], F32)
    sa16 = p.sb("moe_sa", [P, 16 * TB], BF16)
    gated = p.sb("moe_gat", [P, 16 * TB], BF16)
    btmp = p.sb("moe_btmp", [P, 2 * TB], F32)
    val_sb = p.sb("moe_val", [P, 2 * TB], F32)
    ytmp = p.sb("moe_ytmp", [P, 8 * TB], F32)

    bl2r_sb = p.sb("moe_bl2r", [NEXP, D], BF16)
    valb = [p.sb(f"moe_valb{n}", [1, T], BF16) for n in range(NEXP)]
    valblk16 = p.sb("moe_valblk", [NEXP, T], BF16)
    gb = p.group("moe_bl1")

    @p.sec("g")
    def _(g):
        p.streams["r_valst"].wait_all(g)
        gvb = p.group("moe_valblk")
        for e in range(NEXP):
            gvb.issue(g, valb[e][:], p.val_h[e::NEXP, :])
            gvb.issue(g, valblk16[e:e + 1, :], p.val_h[e::NEXP, :])

    @p.sec("q")
    def _(q):
        for n in range(NEXP):
            gb.issue(q, bl1_sb[:, n * 32:(n + 1) * 32], p.bl1_h[n])

    @p.sec("g")
    def _(g):
        gb2 = p.group("moe_bl2r")
        for n in range(NEXP):
            gb2.issue(g, bl2r_sb[n:n + 1, :], p.bl2r_h[n:n + 1, :])

    it_dma = []

    for n in range(NEXP):
        gw = p.group(f"moe_w{n}")

        @p.sec("g")
        def _(g, n=n, gw=gw):
            if n >= 1:
                p.wait(g, f"moe_mmdone{n - 1}")
            for k in range(KD):
                gw.issue(g, wl1_sb[:, k * 4096:(k + 1) * 4096],
                         p.wl1_h[n, k * P:(k + 1) * P, :])
            for k in range(KDI):
                gw.issue(g, wl2_sb[:, k * D:(k + 1) * D],
                         p.wl2_h[n, k * P:(k + 1) * P, :])

        for j in range(NTB):
            it = n * NTB + j

            @p.sec("t")
            def _(t, n=n, j=j, it=it, gw=gw):
                if j == 0:
                    gw.wait_all(t)
                    if n == 0:
                        gb.wait_all(t)
                        p.streams["moe_valblk"].wait_all(t)
                if it >= 1:
                    p.wait(t, f"moe_valcp{it - 1}")
                    p.wait(t, f"moe_gat{it - 1}_15")
                ins = t.matmul(p.ps_t[:, 0:TB], p.ones16_sb[:],
                               valb[n][:, j * TB:(j + 1) * TB],
                               start=True, stop=True)
                p.mark(f"moe_valmm{it}", ins, "t")

            @p.sec("v")
            def _(v, it=it):
                p.wait(v, f"moe_valmm{it}")
                p.mark(f"moe_valcp{it}", v.tensor_copy(
                    out=val_sb[:, 0:TB], in_=p.ps_t[:, 0:TB]), "v")

            for i in range(16):
                par = i % 2
                a_ps = [p.ps_a, p.ps_b][par]
                a_sl = slice(0, TB)
                b_sl = [slice(TB, 2 * TB), slice(2 * TB, 3 * TB)][par]

                @p.sec("t")
                def _(t, n=n, j=j, i=i, it=it, a_ps=a_ps, a_sl=a_sl, b_sl=b_sl):
                    if i >= 2:
                        p.wait(t, f"moe_sa{it}_{i - 2}")
                        p.wait(t, f"moe_gat{it}_{i - 2}")
                    elif it >= 1:
                        p.wait(t, f"moe_sa{it - 1}_{14 + i}")
                        p.wait(t, f"moe_gat{it - 1}_{14 + i}")
                    if i == 0:
                        p.wait(t, f"moe_valcp{it}")
                    for o, ps, psl, tag in ((i, a_ps, a_sl, "a"),
                                            (16 + i, p.ps_t, b_sl, "b")):
                        for k in range(KD):
                            ins = t.matmul(
                                ps[:, psl],
                                wl1_sb[:, k * 4096 + o * P:k * 4096 + (o + 1) * P],
                                p.xn_fm[:, k * T + j * TB:k * T + (j + 1) * TB],
                                start=(k == 0), stop=(k == KD - 1))
                        p.mark(f"moe_mm{tag}{it}_{i}", ins, "t")

                @p.sec("s")
                def _(s, n=n, it=it, i=i, a_ps=a_ps, a_sl=a_sl):
                    p.wait(s, f"moe_mma{it}_{i}")
                    p.mark(f"moe_sa{it}_{i}", s.activation(
                        out=sa16[:, i * TB:(i + 1) * TB],
                        in_=a_ps[:, a_sl], func=AF.Silu,
                        bias=bl1_sb[:, n * 32 + i:n * 32 + i + 1]), "s")

                @p.sec("v")
                def _(v, n=n, it=it, i=i, par=par, b_sl=b_sl):
                    p.wait(v, f"moe_mmb{it}_{i}")
                    v.scalar_tensor_tensor(
                        out=btmp[:, par * TB:(par + 1) * TB],
                        in0=p.ps_t[:, b_sl],
                        scalar=bl1_sb[:, n * 32 + 16 + i:n * 32 + 17 + i],
                        in1=val_sb[:, 0:TB], op0=ALU.add, op1=ALU.mult)
                    p.wait(v, f"moe_sa{it}_{i}")
                    p.mark(f"moe_gat{it}_{i}", v.tensor_tensor(
                        out=gated[:, i * TB:(i + 1) * TB],
                        in0=sa16[:, i * TB:(i + 1) * TB],
                        in1=btmp[:, par * TB:(par + 1) * TB], op=ALU.mult), "v")

            @p.sec("t")
            def _(t, n=n, j=j, it=it):
                p.wait(t, f"moe_gat{it}_15")
                if it >= 1:
                    p.wait(t, f"moe_yev{it - 1}")
                if n == 0:
                    p.streams["moe_bl2r"].wait_all(t)
                for o in range(8):
                    for k in range(KDI):
                        ins = t.matmul(
                            p.ps_y[:, o * TB:(o + 1) * TB],
                            wl2_sb[:, k * D + o * P:k * D + (o + 1) * P],
                            gated[:, k * TB:(k + 1) * TB],
                            start=(k == 0),
                            stop=(k == KDI - 1 and n != 0))
                    if n == 0:
                        ins = t.matmul(
                            p.ps_y[:, o * TB:(o + 1) * TB],
                            bl2r_sb[:, o * P:(o + 1) * P],
                            valblk16[0:NEXP, j * TB:(j + 1) * TB],
                            start=False, stop=True)
                p.mark(f"moe_ymm{it}", ins, "t")
                if j == NTB - 1:
                    p.mark(f"moe_mmdone{n}", t.drain(), "t")

            @p.sec("s")
            def _(s, it=it):
                p.wait(s, f"moe_ymm{it}")
                if it >= 1:
                    st_, use_ = it_dma[it - 1]
                    st_.wait(s, use_)
                for o in range(8):
                    ins = s.activation(
                        out=ytmp[:, o * TB:(o + 1) * TB],
                        in_=p.ps_y[:, o * TB:(o + 1) * TB], func=AF.Copy)
                p.mark(f"moe_yev{it}", s.drain(), "s")

            @p.sec("g")
            def _(g, n=n, j=j, it=it):
                p.wait(g, f"moe_yev{it}")
                st_ = p.stream(f"yac{j}", 1)
                use_ = st_.n
                dst = p.ymoe_h.rearrange("pp (o t) -> pp o t", t=T)[:, :, j * TB:(j + 1) * TB]
                st_.issue(g, dst,
                    ytmp.ap().rearrange("pp (o t) -> pp o t", o=8),
                    accum_op=(ALU.bypass if n == 0 else ALU.add))
                it_dma.append((st_, use_))



def emit_p2b(p):
    HAL = 264
    WXW = 112
    wxs = p.sb("mb_wx", [P, 2 * KDI * WXW], BF16)
    wds = p.sb("mb_wd", [DTR, 2 * DI], BF16)
    wos = p.sb("mb_wo", [P, KDI * D], BF16)
    cwv = p.sb("mb_cw", [P, 2 * KDI * 4], F32)
    cbv = p.sb("mb_cb", [P, 2 * KDI], F32)
    bdv = p.sb("mb_bd", [P, 2 * KDI], F32)
    dsv = p.sb("mb_ds", [P, 2 * KDI], F32)
    asb = p.sb("mb_A", [P, 2 * KDI * NST], F32)
    xzb = p.sb("mb_xzb", [P, 32 * HAL], BF16)
    x16 = p.sb("mb_x16", [P, KDI * TC], BF16)
    o16 = x16
    u16 = p.sb("mb_u16", [P, KDI * TC], BF16)
    dtb = p.sb("mb_dt", [P, KDI * TC], F32)
    sc1 = p.sb("mb_sc1", [P, KDI * TC], BF16)
    dbu = p.sb("mb_dbu", [P, 2 * KDI * TC], BF16)
    dab = p.sb("mb_dab", [P, 2 * KDI * TC], BF16)
    hbf = p.sb("mb_h", [P, KDI * TC], F32)
    yac = p.sb("mb_y", [P, KDI * TC], F32)
    xd16 = p.sb("mb_xd", [DTR, TC], BF16)
    xB16 = p.sb("mb_xB", [NST, TC], BF16)
    xC16 = p.sb("mb_xC", [NST, TC], BF16)
    Bb = p.sb("mb_Bb", [P, 2 * TC], F32)
    Cb = p.sb("mb_Cb", [P, 2 * TC], F32)
    car = p.sb("mb_car", [P, NST * KDI], F32)
    sbb = p.sb("mb_sbb", [P, 8 * TC], F32)
    gld = p.sb("mb_gld", [P, 8 * TC], F32)
    x1t = p.sb("mb_x1t", [P, 2 * D], F32)

    gw = p.group("mb_w")

    @p.sec("g")
    def _(g):
        for d_ in range(2):
            for k in range(KDI):
                gw.issue(g, wxs[:, (d_ * KDI + k) * WXW:(d_ * KDI + k + 1) * WXW],
                         p.wxT_h[d_, k * P:(k + 1) * P, :])
            gw.issue(g, wds[:, d_ * DI:(d_ + 1) * DI], p.wdtT_h[d_])
        for k in range(KDI):
            gw.issue(g, wos[:, k * D:(k + 1) * D], p.w_outT_h[k * P:(k + 1) * P, :])

    gc = p.group("mb_c")

    @p.sec("q")
    def _(q):
        for o_, i_ in ((cwv, p.conv_w_h), (cbv, p.conv_b_h), (bdv, p.bdt_h),
                       (dsv, p.d_skip_h), (asb, p.a_log_h)):
            gc.issue(q, o_[:, 0:o_.shape[1] // 2], i_[0])
            gc.issue(q, o_[:, o_.shape[1] // 2:], i_[1])

    @p.sec("s")
    def _(s):
        gc.wait_all(s)
        p.mark("mb_expA", s.activation(out=asb[:], in_=asb[:], func=AF.Exp), "s")

    @p.sec("v")
    def _(v):
        p.wait(v, "mb_expA")
        p.mark("mb_A", v.tensor_scalar(out=asb[:], in0=asb[:], scalar1=-1.0,
                                       scalar2=None, op0=ALU.mult), "v")

    for d_ in (1, 0):
        xsrc = p.xzr_h if d_ == 1 else p.xz_h

        @p.sec("v")
        def _(v, d_=d_):
            if d_ == 1:
                p.wait(v, "mb_A")
            p.mark(f"mb_car0_{d_}", v.memset(car[:], 0.0), "v")

        for ci in range(NCH):
            cid = f"{d_}_{ci}"
            lo = ci * TC - 3

            if ci == 0:
                @p.sec("v")
                def _(v, d_=d_, cid=cid):
                    if d_ == 0:
                        p.wait(v, f"mb_z{1}_{NCH - 1}")
                    ins = v.memset(xzb.ap().rearrange(
                        "p (dt c) -> p dt c", c=HAL)[:, :, 0:3], 0.0)
                    p.mark(f"mb_halo{cid}", ins, "v")

            @p.sec("q")
            def _(q, d_=d_, ci=ci, cid=cid, lo=lo, xsrc=xsrc):
                st_ = p.stream("mb_xzld", 2)
                if ci == 0:
                    p.wait(q, f"mb_halo{cid}")
                else:
                    p.wait(q, f"mb_conv{d_}_{ci - 1}")
                    p.wait(q, f"mb_z{d_}_{ci - 1}")
                for dt_ in range(32):
                    src_lo = max(lo, 0)
                    ncol = (ci + 1) * TC - src_lo
                    off = 3 if ci == 0 else 0
                    st_.issue(q, xzb[:, dt_ * HAL + off:dt_ * HAL + off + ncol],
                              xsrc[dt_ * P:(dt_ + 1) * P, src_lo:(ci + 1) * TC])

            @p.sec("v")
            def _(v, d_=d_, ci=ci, cid=cid):
                st_ = p.stream("mb_xzld", 2)
                st_.wait_all(v)
                if ci >= 1 or d_ == 0:
                    prev = f"{d_}_{ci - 1}" if ci >= 1 else f"1_{NCH - 1}"
                    p.wait(v, f"mb_woutdone{prev}")
                for dt_ in range(KDI):
                    base = dt_ * HAL
                    cacc = dtb[:, dt_ * TC:(dt_ + 1) * TC]
                    v.tensor_scalar(
                        out=cacc, in0=xzb[:, base:base + TC],
                        scalar1=cwv[:, (d_ * KDI + dt_) * 4:(d_ * KDI + dt_) * 4 + 1],
                        scalar2=cbv[:, d_ * KDI + dt_:d_ * KDI + dt_ + 1],
                        op0=ALU.mult, op1=ALU.add)
                    for j in (1, 2, 3):
                        v.scalar_tensor_tensor(
                            out=cacc, in0=xzb[:, base + j:base + j + TC],
                            scalar=cwv[:, (d_ * KDI + dt_) * 4 + j:(d_ * KDI + dt_) * 4 + j + 1],
                            in1=cacc, op0=ALU.mult, op1=ALU.add)
                p.mark(f"mb_conv{cid}", v.drain(), "v")

            @p.sec("s")
            def _(s, cid=cid):
                p.wait(s, f"mb_conv{cid}")
                for dt_ in range(KDI):
                    s.activation(out=x16[:, dt_ * TC:(dt_ + 1) * TC],
                                 in_=dtb[:, dt_ * TC:(dt_ + 1) * TC], func=AF.Silu)
                p.mark(f"mb_x16{cid}", s.drain(), "s")

            @p.sec("t")
            def _(t, d_=d_, ci=ci, cid=cid):
                if d_ == 1 and ci == 0:
                    gw.wait_all(t)
                p.wait(t, f"mb_x16{cid}")
                if ci >= 1 or d_ == 0:
                    p.wait(t, f"mb_xdblev{d_ if ci else 1}_{ci - 1 if ci else NCH - 1}")
                for k in range(KDI):
                    ins = t.matmul(p.ps_a[0:112, 0:TC],
                                   wxs[:, (d_ * KDI + k) * WXW:(d_ * KDI + k + 1) * WXW],
                                   x16[:, k * TC:(k + 1) * TC],
                                   start=(k == 0), stop=(k == KDI - 1))
                p.mark(f"mb_xdbl{cid}", ins, "t")

            @p.sec("s")
            def _(s, cid=cid):
                p.wait(s, f"mb_xdbl{cid}")
                s.activation(out=xd16[:], in_=p.ps_a[0:DTR, 0:TC], func=AF.Copy)
                s.activation(out=xB16[:], in_=p.ps_a[64:80, 0:TC], func=AF.Copy)
                s.activation(out=xC16[:], in_=p.ps_a[96:112, 0:TC], func=AF.Copy)
                p.mark(f"mb_xdblev{cid}", s.drain(), "s")

            for oi in range(KDI):
                @p.sec("t")
                def _(t, d_=d_, ci=ci, cid=cid, oi=oi):
                    if oi == 0:
                        p.wait(t, f"mb_xdblev{cid}")
                    if oi >= 2:
                        p.wait(t, f"mb_spv{cid}_{oi - 2}")
                    ps = [p.ps_a, p.ps_b][oi % 2]
                    p.mark(f"mb_dtmm{cid}_{oi}", t.matmul(
                        ps[:, 0:TC], wds[:, d_ * DI + oi * P:d_ * DI + (oi + 1) * P],
                        xd16[:], start=True, stop=True), "t")

                @p.sec("s")
                def _(s, d_=d_, ci=ci, cid=cid, oi=oi):
                    p.wait(s, f"mb_dtmm{cid}_{oi}")
                    if oi == 0 and (ci >= 1 or d_ == 0):
                        prevc = f"{d_}_{ci - 1}" if ci >= 1 else f"1_{NCH - 1}"
                        p.wait(s, f"mb_o16{prevc}")
                    ps = [p.ps_a, p.ps_b][oi % 2]
                    p.mark(f"mb_sps{cid}_{oi}", s.activation(
                        out=hbf[:, oi * TC:(oi + 1) * TC], in_=ps[:, 0:TC],
                        func=AF.Identity,
                        bias=bdv[:, d_ * KDI + oi:d_ * KDI + oi + 1]), "s")

                @p.sec("v")
                def _(v, cid=cid, oi=oi):
                    p.wait(v, f"mb_sps{cid}_{oi}")
                    xb = hbf[:, oi * TC:(oi + 1) * TC]
                    r = dtb[:, oi * TC:(oi + 1) * TC]
                    v.tensor_scalar_max(r, xb, 0.0)
                    v.drain()
                    p.mark(f"mb_spv{cid}_{oi}", v.scalar_tensor_tensor(
                        out=xb, in0=r, scalar=-2.0, in1=xb,
                        op0=ALU.mult, op1=ALU.add), "v")

            @p.sec("s")
            def _(s, cid=cid):
                p.wait(s, f"mb_spv{cid}_{KDI - 1}")
                s.activation(out=hbf[:], in_=hbf[:], func=AF.Exp)
                s.drain()
                p.mark(f"mb_spl{cid}", s.activation(
                    out=hbf[:], in_=hbf[:], func=AF.Ln, bias=1.0), "s")

            @p.sec("v")
            def _(v, cid=cid):
                p.wait(v, f"mb_spl{cid}")
                v.tensor_add(dtb[:], dtb[:], hbf[:])
                v.drain()
                p.mark(f"mb_u{cid}", v.tensor_tensor(
                    out=u16[:], in0=dtb[:], in1=x16[:], op=ALU.mult), "v")

            prev = (f"{d_}_{ci - 1}" if ci >= 1
                    else (f"1_{NCH - 1}" if d_ == 0 else None))

            def slotev(base, k, cid=cid, prev=prev):
                if k >= 0:
                    return f"mb_{base}{cid}_{k}"
                if prev is not None:
                    return f"mb_{base}{prev}_{NST + k}"
                return None

            for n in range(NST):
                nid = f"{cid}_{n}"
                pb = n % 2
                dsl = slice(pb * KDI * TC, (pb + 1) * KDI * TC)
                bsl = slice(pb * TC, (pb + 1) * TC)

                @p.sec("t")
                def _(t, cid=cid, nid=nid, n=n, pb=pb, slotev=slotev):
                    if n == 0:
                        p.wait(t, f"mb_xdblev{cid}")
                    ev = slotev("bce", n - 2)
                    if ev:
                        p.wait(t, ev)
                    t.matmul(p.ps_t[:, pb * TC:(pb + 1) * TC],
                             p.onehotb_sb[0:NST, n * P:(n + 1) * P],
                             xB16[:], start=True, stop=True)
                    p.mark(f"mb_bc{nid}", t.matmul(
                        p.ps_t[:, 2 * TC + pb * TC:2 * TC + (pb + 1) * TC],
                        p.onehotb_sb[0:NST, n * P:(n + 1) * P],
                        xC16[:], start=True, stop=True), "t")

                @p.sec("s")
                def _(s, d_=d_, nid=nid, cid=cid, n=n, pb=pb, bsl=bsl, dsl=dsl,
                      slotev=slotev):
                    p.wait(s, f"mb_bc{nid}")
                    for ev in (slotev("dbu", n - 2), slotev("yac", n - 2)):
                        if ev:
                            p.wait(s, ev)
                    s.activation(out=Bb[:, bsl], in_=p.ps_t[:, pb * TC:(pb + 1) * TC],
                                 func=AF.Copy)
                    p.mark(f"mb_bce{nid}", s.activation(
                        out=Cb[:, bsl],
                        in_=p.ps_t[:, 2 * TC + pb * TC:2 * TC + (pb + 1) * TC],
                        func=AF.Copy), "s")
                    if n == 0:
                        p.wait(s, f"mb_u{cid}")
                    ev = slotev("scan", n - 2)
                    if ev:
                        p.wait(s, ev)
                    for dt_ in range(KDI):
                        ins = s.activation(
                            out=dab[:, pb * KDI * TC + dt_ * TC:
                                    pb * KDI * TC + (dt_ + 1) * TC],
                            in_=dtb[:, dt_ * TC:(dt_ + 1) * TC], func=AF.Exp,
                            scale=asb[:, (d_ * KDI + dt_) * NST + n:
                                      (d_ * KDI + dt_) * NST + n + 1])
                    p.mark(f"mb_dA{nid}", ins, "s")

                @p.sec("g")
                def _(g, nid=nid, cid=cid, n=n, pb=pb, bsl=bsl, dsl=dsl,
                      slotev=slotev):
                    p.wait(g, f"mb_bce{nid}")
                    if n < 2:
                        p.wait(g, f"mb_u{cid}")
                    ev = slotev("scan", n - 2)
                    if ev:
                        p.wait(g, ev)
                    p.mark(f"mb_dbu{nid}", g.tensor_tensor(
                        out=dbu[:, dsl].rearrange("p (d t) -> p d t", t=TC),
                        in0=u16.ap().rearrange("p (d t) -> p d t", t=TC),
                        in1=Bb[:, bsl].unsqueeze(1).broadcast_to([P, KDI, TC]),
                        op=ALU.mult), "g")

                @p.sec("v")
                def _(v, nid=nid, cid=cid, n=n, pb=pb, dsl=dsl):
                    if n >= 1:
                        qb = (n - 1) % 2
                        v.tensor_tensor(
                            out=sc1.ap().rearrange("p (d t) -> p d t", t=TC),
                            in0=hbf.ap().rearrange("p (d t) -> p d t", t=TC),
                            in1=Cb[:, qb * TC:(qb + 1) * TC]
                                .unsqueeze(1).broadcast_to([P, KDI, TC]),
                            op=ALU.mult)
                        v.drain()
                        if n == 1:
                            ins = v.tensor_copy(out=yac[:], in_=sc1[:])
                        else:
                            ins = v.tensor_add(yac[:], yac[:], sc1[:])
                        p.mark(f"mb_yac{cid}_{n - 1}", ins, "v")
                    p.wait(v, f"mb_dA{nid}")
                    p.wait(v, f"mb_dbu{nid}")
                    for dt_ in range(KDI):
                        v.tensor_tensor_scan(
                            out=hbf[:, dt_ * TC:(dt_ + 1) * TC],
                            data0=dab[:, pb * KDI * TC + dt_ * TC:
                                      pb * KDI * TC + (dt_ + 1) * TC],
                            data1=dbu[:, pb * KDI * TC + dt_ * TC:
                                      pb * KDI * TC + (dt_ + 1) * TC],
                            initial=car[:, n * KDI + dt_:n * KDI + dt_ + 1],
                            op0=ALU.mult, op1=ALU.add)
                    v.drain()
                    p.mark(f"mb_scan{cid}_{n}", v.tensor_copy(
                        out=car[:, n * KDI:(n + 1) * KDI],
                        in_=hbf[:, TC - 1::TC]), "v")

            @p.sec("v")
            def _(v, cid=cid):
                qb = (NST - 1) % 2
                v.tensor_tensor(
                    out=sc1.ap().rearrange("p (d t) -> p d t", t=TC),
                    in0=hbf.ap().rearrange("p (d t) -> p d t", t=TC),
                    in1=Cb[:, qb * TC:(qb + 1) * TC]
                        .unsqueeze(1).broadcast_to([P, KDI, TC]),
                    op=ALU.mult)
                v.drain()
                p.mark(f"mb_yac{cid}_{NST - 1}", v.tensor_add(
                    yac[:], yac[:], sc1[:]), "v")

            @p.sec("s")
            def _(s, cid=cid):
                p.wait(s, f"mb_yac{cid}_{NST - 1}")
                for dt_ in range(KDI):
                    s.activation(out=hbf[:, dt_ * TC:(dt_ + 1) * TC],
                                 in_=xzb[:, (16 + dt_) * HAL + 3:(16 + dt_) * HAL + 3 + TC],
                                 func=AF.Silu)
                p.mark(f"mb_z{cid}", s.drain(), "s")

            @p.sec("v")
            def _(v, d_=d_, cid=cid):
                p.wait(v, f"mb_z{cid}")
                v.drain()
                for dt_ in range(KDI):
                    sl = slice(dt_ * TC, (dt_ + 1) * TC)
                    v.scalar_tensor_tensor(
                        out=yac[:, sl], in0=x16[:, sl],
                        scalar=dsv[:, d_ * KDI + dt_:d_ * KDI + dt_ + 1],
                        in1=yac[:, sl], op0=ALU.mult, op1=ALU.add)
                v.drain()
                p.mark(f"mb_o16{cid}", v.tensor_tensor(
                    out=o16[:], in0=yac[:], in1=hbf[:], op=ALU.mult), "v")

            @p.sec("t")
            def _(t, d_=d_, ci=ci, cid=cid):
                p.wait(t, f"mb_o16{cid}")
                if d_ == 1 and ci >= 1:
                    p.wait(t, f"mb_sev{d_}_{ci - 1}")
                elif d_ == 0:
                    p.wait(t, f"mb_x1{d_}_{ci - 1}" if ci >= 1 else f"mb_sev1_{NCH - 1}")
                for o in range(KD):
                    for k in range(KDI):
                        ins = t.matmul(p.ps_y[:, o * TC:(o + 1) * TC],
                                       wos[:, k * D + o * P:k * D + (o + 1) * P],
                                       o16[:, k * TC:(k + 1) * TC],
                                       start=(k == 0), stop=(k == KDI - 1))
                p.mark(f"mb_smm{cid}", ins, "t")

            if d_ == 1:
                @p.sec("v")
                def _(v, ci=ci, cid=cid):
                    p.wait(v, f"mb_smm{cid}")
                    if ci >= 2:
                        p.stream("mb_sbst", 2).wait(v, ci - 2)
                    for o in range(KD):
                        v.tensor_copy(
                            out=sbb[:, o * TC:(o + 1) * TC],
                            in_=p.ps_y[:, (o + 1) * TC - 1:o * TC - 1 if o else None:-1])
                    p.mark(f"mb_sev{cid}", v.drain(), "v")

                @p.sec("q")
                def _(q, ci=ci, cid=cid):
                    p.wait(q, f"mb_sev{cid}")
                    dst = p.sb_h.rearrange("pp (o t) -> pp o t", t=T)[
                        :, :, T - (ci + 1) * TC:T - ci * TC]
                    p.stream("mb_sbst", 2).issue(
                        q, dst, sbb.ap().rearrange("p (o t) -> p o t", o=KD))
            else:
                @p.sec("q")
                def _(q, ci=ci, cid=cid):
                    st1 = p.stream("mb_sbld", 2)
                    st2 = p.stream("mb_gld", 2)
                    if ci == 0:
                        p.streams["mb_sbst"].wait_all(q)
                        for st3 in [p.streams.get(f"yac{j}") for j in range(NTB)]:
                            if st3 is not None:
                                st3.wait_all(q)
                    if ci >= 1:
                        p.wait(q, f"mb_gate{d_}_{ci - 1}")
                    st1.issue(q, sbb.ap().rearrange("p (o t) -> p o t", o=KD),
                              p.sb_h.rearrange("pp (o t) -> pp o t", t=T)[
                                  :, :, ci * TC:(ci + 1) * TC])
                    st2.issue(q, gld.ap().rearrange("p (o t) -> p o t", o=KD),
                              p.ymoe_h.rearrange("pp (o t) -> pp o t", t=T)[
                                  :, :, ci * TC:(ci + 1) * TC])
                    st3 = p.stream("mb_x0ld", 2)
                    if ci >= 1:
                        p.stream("mb_x1st", 2).wait(q, ci - 1)
                    st3.issue(q, x1t.ap().rearrange("pp (a dd) -> pp a dd", a=2),
                              p.x0_h[ci * TC:(ci + 1) * TC, :].rearrange(
                                  "(a pp) dd -> pp a dd", pp=P))

                @p.sec("s")
                def _(s, ci=ci, cid=cid):
                    p.stream("mb_gld", 2).wait(s, ci)
                    p.mark(f"mb_g{cid}", s.activation(
                        out=gld[:], in_=gld[:], func=AF.Silu), "s")

                @p.sec("v")
                def _(v, ci=ci, cid=cid):
                    p.wait(v, f"mb_smm{cid}")
                    p.stream("mb_sbld", 2).wait(v, ci)
                    p.wait(v, f"mb_g{cid}")
                    v.tensor_add(sbb[:], p.ps_y[:, 0:KD * TC], sbb[:])
                    v.drain()
                    v.scalar_tensor_tensor(out=sbb[:], in0=sbb[:], scalar=0.5,
                                           in1=gld[:], op0=ALU.mult, op1=ALU.mult)
                    p.mark(f"mb_sg{cid}", v.drain(), "v")

                @p.sec("t")
                def _(t, ci=ci, cid=cid):
                    p.wait(t, f"mb_sg{cid}")
                    if ci >= 1:
                        p.wait(t, f"mb_x1{d_}_{ci - 1}")
                    for h in range(2):
                        for o in range(KD):
                            idx = h * KD + o
                            ins = t.transpose(
                                p.ps_y[:, idx * P:(idx + 1) * P],
                                sbb[:, o * TC + h * P:o * TC + (h + 1) * P],
                                p.ident_sb[:])
                            if idx % 8 == 7:
                                p.mark(f"mb_str{cid}_{idx // 8}", ins, "t")

                @p.sec("v")
                def _(v, ci=ci, cid=cid):
                    p.stream("mb_x0ld", 2).wait(v, ci)
                    for h in range(2):
                        p.wait(v, f"mb_str{cid}_{h}")
                    for h in range(2):
                        for o in range(KD):
                            idx = h * KD + o
                            v.tensor_tensor(
                                out=x1t[:, h * D + o * P:h * D + (o + 1) * P],
                                in0=p.ps_y[:, idx * P:(idx + 1) * P],
                                in1=x1t[:, h * D + o * P:h * D + (o + 1) * P],
                                op=ALU.add)
                    p.mark(f"mb_x1{cid}", v.drain(), "v")

                @p.sec("q")
                def _(q, ci=ci, cid=cid):
                    p.wait(q, f"mb_x1{cid}")
                    st_ = p.stream("mb_x1st", 2)
                    st_.issue(q, p.x1_h[ci * TC:(ci + 1) * TC, :].rearrange(
                        "(a pp) dd -> pp a dd", pp=P),
                        x1t.ap().rearrange("pp (a dd) -> pp a dd", a=2))
                    q.sem_inc(p.sem["q"], 1)
                    p.cnt["q"] += 1
                    p.ev.setdefault(f"mb_gate{cid}", []).append(("q", p.cnt["q"]))

            @p.sec("t")
            def _(t, d_=d_, ci=ci, cid=cid):
                p.mark(f"mb_woutdone{cid}", t.drain(), "t")

        p.barrier()



def emit_l1a(p):
    x1t = p.sb("la_x1", [P, 2 * D], F32)
    xnt = p.sb("la_xn", [P, 2 * D], F32)
    lnt = p.sb("la_ln", [P, 2 * D], F32)
    st = p.sb("la_st", [P, 32], F32)
    lg = p.sb("la_lg", [P, KD], F32)
    lb = p.sb("la_lb", [P, KD], F32)
    fb = p.sb("la_fb", [P, KD], F32)
    wf = p.sb("la_wf", [P, KD * D], BF16)
    vr = p.sb("la_vr", [P, KD * 32], BF16)
    fc = p.sb("la_fc", [32, D], BF16)
    u16 = p.sb("la_u16", [32, TC], BF16)
    gsl = p.sb("la_gsl", [P, KD * TC], BF16)

    gw = p.group("la_w")

    @p.sec("g")
    def _(g):
        for k in range(KD):
            gw.issue(g, wf[:, k * D:(k + 1) * D], p.fw_h[k * P:(k + 1) * P, :])
            gw.issue(g, vr[:, k * 32:(k + 1) * 32], p.fvr_h[k * P:(k + 1) * P, :])
        gw.issue(g, fc[:], p.fcT_h)

    gc = p.group("la_c")

    @p.sec("q")
    def _(q):
        gc.issue(q, lg[:], p.ln1_g_h)
        gc.issue(q, lb[:], p.ln1_b_h)
        gc.issue(q, fb[:], p.fb_h)

    for ci in range(NCH):
        cid = f"l1a{ci}"

        @p.sec("q")
        def _(q, ci=ci):
            st_ = p.stream("la_x1ld", 2)
            if ci >= 1:
                p.wait(q, f"la_apl{ci - 1}")
            st_.issue(q, x1t.ap().rearrange("pp (a dd) -> pp a dd", a=2),
                      p.x1_h[ci * TC:(ci + 1) * TC, :].rearrange(
                          "(a pp) dd -> pp a dd", pp=P))

        for h in range(2):
            i = 2 * ci + h

            @p.sec("v")
            def _(v, ci=ci, i=i):
                p.stream("la_x1ld", 2).wait(v, ci)
                if i >= 2:
                    p.wait(v, f"la_tr{i - 2}_3")
            p.ln_stats_apply(i, x1t[:, h * D:(h + 1) * D],
                             xnt[:, h * D:(h + 1) * D], st[:, 0:16], "la1")
            p.ln_stats_apply(i, xnt[:, h * D:(h + 1) * D],
                             lnt[:, h * D:(h + 1) * D], st[:, 16:32], "la2")

            @p.sec("t")
            def _(t, i=i, h=h):
                p.wait(t, f"la2_xn{i}")
                if i >= 1:
                    p.wait(t, f"la_ev{i - 1}")
                for k in range(KD):
                    ins = t.transpose(p.ps_y[:, k * P:(k + 1) * P],
                                      xnt[:, h * D + k * P:h * D + (k + 1) * P],
                                      p.ident_sb[:])
                    if k == KD - 1:
                        p.mark(f"la_tr{i}_0", ins, "t")
                for k in range(KD):
                    ins = t.transpose(p.ps_y[:, (KD + k) * P:(KD + k + 1) * P],
                                      lnt[:, h * D + k * P:h * D + (k + 1) * P],
                                      p.ident_sb[:])
                    if k == KD - 1:
                        p.mark(f"la_tr{i}_3", ins, "t")

            @p.sec("s")
            def _(s, i=i):
                p.wait(s, f"la_tr{i}_0")
                for k in range(KD):
                    s.activation(out=p.xn1_fm_sb[:, k * T + i * P:k * T + (i + 1) * P],
                                 in_=p.ps_y[:, k * P:(k + 1) * P], func=AF.Copy)
                p.mark(f"la_sev{i}", s.drain(), "s")

            @p.sec("v")
            def _(v, i=i):
                p.wait(v, f"la_tr{i}_3")
                if i == 0:
                    gc.wait_all(v)
                for k in range(KD):
                    v.tensor_scalar(
                        out=p.ln1_fm_sb[:, k * T + i * P:k * T + (i + 1) * P],
                        in0=p.ps_y[:, (KD + k) * P:(KD + k + 1) * P],
                        scalar1=lg[:, k:k + 1], scalar2=lb[:, k:k + 1],
                        op0=ALU.mult, op1=ALU.add)
                p.wait(v, f"la_sev{i}")
                p.mark(f"la_ev{i}", v.drain(), "v")
                if i % 2 == 1:
                    p.mark(f"la_apl{i // 2}", v.drain(), "v")

        @p.sec("t")
        def _(t, ci=ci, cid=cid):
            if ci == 0:
                gw.wait_all(t)
            p.wait(t, f"la_ev{2 * ci + 1}")
            if ci >= 1:
                p.wait(t, f"la_uev{ci - 1}")
            for k in range(KD):
                ins = t.matmul(p.ps_t[0:32, 0:TC], vr[:, k * 32:(k + 1) * 32],
                               p.xn1_fm_sb[:, k * T + ci * TC:k * T + (ci + 1) * TC],
                               start=(k == 0), stop=(k == KD - 1))
            p.mark(f"la_umm{cid}", ins, "t")

        @p.sec("s")
        def _(s, ci=ci, cid=cid):
            p.wait(s, f"la_umm{cid}")
            p.mark(f"la_uev{ci}", s.activation(
                out=u16[:], in_=p.ps_t[0:32, 0:TC], func=AF.Copy), "s")

        for o in range(KD):
            @p.sec("t")
            def _(t, ci=ci, cid=cid, o=o):
                if o == 0:
                    p.wait(t, f"la_uev{ci}")
                ps = [p.ps_a, p.ps_b][o % 2]
                if o >= 2:
                    p.wait(t, f"la_g1s{cid}_{o - 2}")
                elif ci >= 1:
                    p.wait(t, f"la_g1s" + f"l1a{ci - 1}" + f"_{KD - 2 + o}")
                for k in range(KD):
                    t.matmul(ps[:, 0:TC], wf[:, k * D + o * P:k * D + (o + 1) * P],
                             p.xn1_fm_sb[:, k * T + ci * TC:k * T + (ci + 1) * TC],
                             start=(k == 0), stop=False)
                ins = t.matmul(ps[:, 0:TC], fc[:, o * P:(o + 1) * P], u16[:],
                               start=False, stop=True)
                p.mark(f"la_g1mm{cid}_{o}", ins, "t")

            @p.sec("s")
            def _(s, ci=ci, cid=cid, o=o):
                p.wait(s, f"la_g1mm{cid}_{o}")
                if o == 0 and ci >= 1:
                    p.stream("la_g1st", 2).wait(s, ci - 1)
                ps = [p.ps_a, p.ps_b][o % 2]
                p.mark(f"la_g1s{cid}_{o}", s.activation(
                    out=gsl[:, o * TC:(o + 1) * TC], in_=ps[:, 0:TC],
                    func=AF.Silu, bias=fb[:, o:o + 1]), "s")

        @p.sec("q")
        def _(q, ci=ci, cid=cid):
            p.wait(q, f"la_g1s{cid}_{KD - 1}")
            p.stream("la_g1st", 2).issue(
                q, p.g1_h.rearrange("pp (o t) -> pp o t", t=T)[
                    :, :, ci * TC:(ci + 1) * TC],
                gsl.ap().rearrange("pp (o t) -> pp o t", o=KD))


def emit_l1b(p):
    wqkv = p.sb("lb_wqkv", [P, 3 * KD * 64], BF16)
    wo = p.sb("lb_wo", [P, KD * D], BF16)
    q16 = p.sb("lb_q", [64, T], BF16)
    k16 = p.sb("lb_k", [64, T], BF16)
    ktm = p.sb("lb_ktm", [P, NACH * 64], BF16)
    vtm = p.sb("lb_vtm", [P, NACH * 64], BF16)
    sc16 = p.sb("lb_sc", [P, P], BF16)
    mt16 = p.sb("lb_mt", [64, 64], BF16)
    x1ic = p.sb("lb_x1i", [P, KD * TC], F32)
    yall = p.sb("lb_yall", [P, KD * T], BF16)

    gwo = p.group("lb_wo")

    @p.sec("g")
    def _(g):
        for k in range(KD):
            gwo.issue(g, wo[:, k * D:(k + 1) * D], p.wo_h[k * P:(k + 1) * P, :])

    for hh in range(HEADS):
        hid_ = f"h{hh}"
        gq = p.group(f"lb_w{hh}")

        @p.sec("g")
        def _(g, hh=hh, gq=gq):
            if hh >= 1:
                p.wait(g, f"lb_proj{hh - 1}")
            for k in range(KD):
                gq.issue(g, wqkv[:, k * 64:(k + 1) * 64],
                         p.wq_h[k * P:(k + 1) * P, hh * 64:(hh + 1) * 64])
                gq.issue(g, wqkv[:, (KD + k) * 64:(KD + k + 1) * 64],
                         p.wk_h[k * P:(k + 1) * P, hh * 64:(hh + 1) * 64])
                gq.issue(g, wqkv[:, (2 * KD + k) * 64:(2 * KD + k + 1) * 64],
                         p.wv_h[k * P:(k + 1) * P, hh * 64:(hh + 1) * 64])

        for which in range(2):
            @p.sec("t")
            def _(t, hh=hh, gq=gq, hid_=hid_, which=which):
                if which == 0:
                    gq.wait_all(t)
                    if hh >= 1:
                        p.wait(t, f"lb_yev{f'h{hh - 1}'}_{NACH - 1}")
                if which >= 1:
                    for sc_ in range(4):
                        p.wait(t, f"lb_pe{hid_}_{sc_}_{which - 1}")
                for sc_ in range(4):
                    for k in range(KD):
                        ins = t.matmul(
                            p.ps_y[0:64, sc_ * 512:(sc_ + 1) * 512],
                            wqkv[:, (which * KD + k) * 64:(which * KD + k + 1) * 64],
                            p.ln1_fm_sb[:, k * T + sc_ * 512:k * T + (sc_ + 1) * 512],
                            start=(k == 0), stop=(k == KD - 1))
                    p.mark(f"lb_p{hid_}_{sc_}_{which}", ins, "t")

            for sc_ in range(4):
                @p.sec("s")
                def _(s, hid_=hid_, sc_=sc_, which=which):
                    p.wait(s, f"lb_p{hid_}_{sc_}_{which}")
                    dst = [q16, k16][which][:, sc_ * 512:(sc_ + 1) * 512]
                    ins = s.activation(out=dst, in_=p.ps_y[0:64, sc_ * 512:(sc_ + 1) * 512],
                                       func=AF.Copy)
                    p.mark(f"lb_pe{hid_}_{sc_}_{which}", ins, "s")

        for half in range(2):
            for grp in range(2):
                bank = half * 2 + grp

                @p.sec("t")
                def _(t, hid_=hid_, half=half, grp=grp, bank=bank):
                    p.wait(t, f"lb_pe{hid_}_{bank}_1")
                    for c8 in range(8):
                        c = grp * 8 + c8
                        for k in range(KD):
                            ins = t.matmul(
                                p.ps_y[:, bank * 512 + c8 * 64:bank * 512 + (c8 + 1) * 64],
                                p.ln1_fm_sb[:, k * T + c * ACH:k * T + (c + 1) * ACH],
                                wqkv[:, ((1 + half) * KD + k) * 64:((1 + half) * KD + k + 1) * 64],
                                start=(k == 0), stop=(k == KD - 1))
                    p.mark(f"lb_tm{hid_}_{half}_{grp}", ins, "t")

                @p.sec("s")
                def _(s, hid_=hid_, half=half, grp=grp, bank=bank):
                    p.wait(s, f"lb_tm{hid_}_{half}_{grp}")
                    p.mark(f"lb_tme{hid_}_{half}_{grp}", s.activation(
                        out=[ktm, vtm][half][:, grp * 512:(grp + 1) * 512],
                        in_=p.ps_y[:, bank * 512:(bank + 1) * 512],
                        func=AF.Copy), "s")

        @p.sec("v")
        def _(v, hid_=hid_):
            p.mark(f"lb_mt0{hid_}", v.memset(mt16[:], 0.0), "v")

        for c in range(NACH):
            ccid = f"{hid_}_{c}"
            csl = slice(c * ACH, (c + 1) * ACH)

            @p.sec("t")
            def _(t, hh=hh, c=c, ccid=ccid, csl=csl, hid_=hid_):
                if c == 0:
                    p.wait(t, f"lb_tme{hid_}_1_1")
                if c >= 1:
                    p.wait(t, f"lb_scev{hid_}_{c - 1}")
                p.mark(f"lb_sc{ccid}", t.matmul(
                    p.ps_a[:, 0:ACH], k16[:, csl], q16[:, csl],
                    start=True, stop=True), "t")

            @p.sec("v")
            def _(v, ccid=ccid, hid_=hid_, c=c):
                p.wait(v, f"lb_sc{ccid}")
                p.mark(f"lb_scev{hid_}_{c}", v.tensor_tensor(
                    out=sc16[:], in0=p.ps_a[:, 0:ACH], in1=p.triuT_sb[:],
                    op=ALU.mult), "v")

            @p.sec("t")
            def _(t, c=c, ccid=ccid, csl=csl, hid_=hid_):
                p.wait(t, f"lb_mtcp{hid_}_{c - 1}" if c >= 1 else f"lb_mt0{hid_}")
                if c >= 1:
                    p.wait(t, f"lb_yev{hid_}_{c - 1}")
                t.matmul(p.ps_t[0:64, 512:512 + ACH], mt16[:], q16[:, csl],
                         start=True, stop=False)
                p.wait(t, f"lb_scev{hid_}_{c}")
                p.mark(f"lb_y{ccid}", t.matmul(
                    p.ps_t[0:64, 512:512 + ACH], vtm[:, c * 64:(c + 1) * 64], sc16[:],
                    start=False, stop=True), "t")
                p.mark(f"lb_mtmm{ccid}", t.matmul(
                    p.ps_t[0:64, 0:64], ktm[:, c * 64:(c + 1) * 64],
                    vtm[:, c * 64:(c + 1) * 64],
                    start=(c == 0), stop=(c == NACH - 1)), "t")

            @p.sec("v")
            def _(v, ccid=ccid, hid_=hid_, c=c):
                p.wait(v, f"lb_mtmm{ccid}")
                p.mark(f"lb_mtcp{hid_}_{c}", v.tensor_copy(
                    out=mt16[:], in_=p.ps_t[0:64, 0:64]), "v")

            @p.sec("s")
            def _(s, hh=hh, c=c, ccid=ccid, hid_=hid_):
                p.wait(s, f"lb_y{ccid}")
                p.mark(f"lb_yev{hid_}_{c}", s.activation(
                    out=yall[(hh % 2) * 64:(hh % 2) * 64 + 64,
                             (hh // 2) * T + c * ACH:(hh // 2) * T + (c + 1) * ACH],
                    in_=p.ps_t[0:64, 512:512 + ACH], func=AF.Copy), "s")

        @p.sec("t")
        def _(t, hh=hh, hid_=hid_):
            p.mark(f"lb_proj{hh}", t.drain(), "t")

    for ci in range(NCH):
        cid = f"lwo{ci}"

        @p.sec("t")
        def _(t, ci=ci, cid=cid):
            if ci == 0:
                gwo.wait_all(t)
                p.wait(t, f"lb_yev{f'h{HEADS - 1}'}_{NACH - 1}")
            if ci >= 1:
                p.wait(t, f"lb_x1iev{ci - 1}")
            for o in range(KD):
                for k in range(KD):
                    ins = t.matmul(p.ps_y[:, o * TC:(o + 1) * TC],
                                   wo[:, k * D + o * P:k * D + (o + 1) * P],
                                   yall[:, k * T + ci * TC:k * T + (ci + 1) * TC],
                                   start=(k == 0), stop=(k == KD - 1))
            p.mark(f"lb_womm{cid}", ins, "t")

        @p.sec("v")
        def _(v, ci=ci, cid=cid):
            p.wait(v, f"lb_womm{cid}")
            if ci >= 2:
                p.stream("lb_x1ist", 2).wait(v, ci - 2)
            for o in range(KD):
                v.tensor_tensor(out=x1ic[:, o * TC:(o + 1) * TC],
                                in0=p.ps_y[:, o * TC:(o + 1) * TC],
                                in1=p.xn1_fm_sb[:, o * T + ci * TC:o * T + (ci + 1) * TC],
                                op=ALU.add)
            p.mark(f"lb_x1iev{ci}", v.drain(), "v")

        @p.sec("q")
        def _(q, ci=ci):
            p.wait(q, f"lb_x1iev{ci}")
            p.stream("lb_x1ist", 2).issue(
                q, p.x1i_h.rearrange("pp (o t) -> pp o t", t=T)[
                    :, :, ci * TC:(ci + 1) * TC],
                x1ic.ap().rearrange("pp (o t) -> pp o t", o=KD))


def emit_l1c(p):
    w1 = p.sb("lc_w1", [P, KD * HID], BF16)
    w2 = p.sb("lc_w2", [P, (HID // P) * D], BF16)
    b1v = p.sb("lc_b1", [P, HID // P], F32)
    b2v = p.sb("lc_b2", [P, KD], F32)
    g2v = p.sb("lc_g2", [P, KD], F32)
    bb2 = p.sb("lc_bb2", [P, KD], F32)
    h16 = p.h16_sb
    hid16 = p.sb("lc_hid", [P, (HID // P) * TC], BF16)
    x1ic = p.sb("lc_x1i", [P, KD * TC], F32)
    sqc = p.sb("lc_sq", [P, KD * TC], BF16)
    row = p.sb("lc_row", [1, 4 * TC], F32)
    onesc16 = p.sb("lc_onesc", [P, 1], BF16)
    gl = p.sb("lc_gl", [P, KD * TC], BF16)

    gc = p.group("lc_c")

    @p.sec("q")
    def _(q):
        gc.issue(q, g2v[:], p.ln2_g_h)
        gc.issue(q, bb2[:], p.ln2_b_h)
        gc.issue(q, onesc16[:], p.onesc16_c)

    for ci in range(NCH):
        cid = f"ln2{ci}"

        @p.sec("q")
        def _(q, ci=ci):
            st_ = p.stream("lc_x1ild", 2)
            if ci >= 1:
                p.wait(q, f"lc_app{ci - 1}")
            st_.issue(q, x1ic.ap().rearrange("pp (o t) -> pp o t", o=KD),
                      p.x1i_h.rearrange("pp (o t) -> pp o t", t=T)[
                          :, :, ci * TC:(ci + 1) * TC])

        @p.sec("s")
        def _(s, ci=ci, cid=cid):
            if ci == 0:
                p.streams["lb_x1ist"].wait_all(s)
            p.stream("lc_x1ild", 2).wait(s, ci)
            for o in range(KD):
                s.activation(out=sqc[:, o * TC:(o + 1) * TC],
                             in_=x1ic[:, o * TC:(o + 1) * TC], func=AF.Square)
            p.mark(f"lc_sq{cid}", s.drain(), "s")

        @p.sec("v")
        def _(v, ci=ci, cid=cid):
            p.stream("lc_x1ild", 2).wait(v, ci)
            if ci >= 1:
                p.wait(v, f"lc_app{ci - 1}")
            p.mark(f"lc_cp{cid}", v.tensor_copy(out=gl[:], in_=x1ic[:]), "v")

        @p.sec("t")
        def _(t, ci=ci, cid=cid):
            p.wait(t, f"lc_sq{cid}")
            if ci == 0:
                gc.wait_all(t)
            if ci >= 1:
                p.wait(t, f"lc_app{ci - 1}")
            p.wait(t, f"lc_cp{cid}")
            for k in range(KD):
                t.matmul(p.ps_t[0:1, 0:TC], onesc16[:], gl[:, k * TC:(k + 1) * TC],
                         start=(k == 0), stop=(k == KD - 1))
            for k in range(KD):
                ins = t.matmul(p.ps_t[0:1, 512:512 + TC], onesc16[:],
                               sqc[:, k * TC:(k + 1) * TC],
                               start=(k == 0), stop=(k == KD - 1))
            p.mark(f"lc_sum{cid}", ins, "t")

        @p.sec("v")
        def _(v, ci=ci, cid=cid):
            p.wait(v, f"lc_sum{cid}")
            v.tensor_scalar(out=row[:, 0:TC], in0=p.ps_t[0:1, 0:TC],
                            scalar1=1.0 / D, scalar2=None, op0=ALU.mult)
            v.tensor_scalar(out=row[:, TC:2 * TC], in0=p.ps_t[0:1, 512:512 + TC],
                            scalar1=1.0 / D, scalar2=None, op0=ALU.mult)
            v.drain()
            v.tensor_tensor(out=row[:, 2 * TC:3 * TC], in0=row[:, 0:TC],
                            in1=row[:, 0:TC], op=ALU.mult)
            v.drain()
            v.tensor_sub(row[:, TC:2 * TC], row[:, TC:2 * TC], row[:, 2 * TC:3 * TC])
            v.drain()
            p.mark(f"lc_var{cid}", v.tensor_scalar(
                out=row[:, TC:2 * TC], in0=row[:, TC:2 * TC], scalar1=1.0,
                scalar2=EPS, op0=ALU.mult, op1=ALU.add), "v")

        @p.sec("s")
        def _(s, cid=cid):
            p.wait(s, f"lc_var{cid}")
            s.activation(out=row[:, 3 * TC:4 * TC], in_=row[:, TC:2 * TC], func=AF.Ln)
            s.drain()
            p.mark(f"lc_rsq{cid}", s.activation(
                out=row[:, 3 * TC:4 * TC], in_=row[:, 3 * TC:4 * TC],
                func=AF.Exp, scale=-0.5), "s")

        @p.sec("t")
        def _(t, cid=cid):
            p.wait(t, f"lc_rsq{cid}")
            t.matmul(p.ps_a[:, 0:TC], p.ones_sb[:], row[:, 0:TC],
                     start=True, stop=True)
            p.mark(f"lc_bc{cid}", t.matmul(
                p.ps_b[:, 0:TC], p.ones_sb[:], row[:, 3 * TC:4 * TC],
                start=True, stop=True), "t")

        @p.sec("v")
        def _(v, ci=ci, cid=cid):
            p.wait(v, f"lc_bc{cid}")
            for o in range(KD):
                sl = slice(o * TC, (o + 1) * TC)
                v.tensor_tensor(out=x1ic[:, sl], in0=x1ic[:, sl],
                                in1=p.ps_a[:, 0:TC], op=ALU.subtract)
            v.drain()
            for o in range(KD):
                sl = slice(o * TC, (o + 1) * TC)
                v.tensor_tensor(out=x1ic[:, sl], in0=x1ic[:, sl],
                                in1=p.ps_b[:, 0:TC], op=ALU.mult)
            v.drain()
            for o in range(KD):
                sl = slice(o * TC, (o + 1) * TC)
                v.tensor_scalar(
                    out=h16[:, o * T + ci * TC:o * T + (ci + 1) * TC],
                    in0=x1ic[:, sl], scalar1=g2v[:, o:o + 1],
                    scalar2=bb2[:, o:o + 1], op0=ALU.mult, op1=ALU.add)
            p.mark(f"lc_app{ci}", v.drain(), "v")

    for lv in range(LEV):
        gw = p.group(f"lc_w{lv}")

        @p.sec("g")
        def _(g, lv=lv, gw=gw):
            if lv >= 1:
                p.wait(g, f"lc_lvl{lv - 1}")
            for k in range(KD):
                gw.issue(g, w1[:, k * HID:(k + 1) * HID],
                         p.cw1_h[lv, k * P:(k + 1) * P, :])
            for k in range(HID // P):
                gw.issue(g, w2[:, k * D:(k + 1) * D],
                         p.cw2_h[lv, k * P:(k + 1) * P, :])

        gcb = p.group(f"lc_cb{lv}")

        @p.sec("q")
        def _(q, lv=lv, gcb=gcb):
            gcb.issue(q, b1v[:], p.cb1_h[lv])
            gcb.issue(q, b2v[:], p.cb2_h[lv])

        for ci in range(NCH):
            cid = f"cms{lv}_{ci}"

            for o in range(HID // P):
                @p.sec("t")
                def _(t, lv=lv, ci=ci, cid=cid, gw=gw, o=o):
                    if o == 0 and ci == 0:
                        gw.wait_all(t)
                        if lv == 0:
                            p.wait(t, f"lc_app{NCH - 1}")
                    ps = [p.ps_a, p.ps_b][o % 2]
                    if o >= 2:
                        p.wait(t, f"lc_ge{cid}_{o - 2}")
                    elif ci >= 1:
                        p.wait(t, f"lc_ge{f'cms{lv}_{ci - 1}'}_{HID // P - 2 + o}")
                    elif lv >= 1 and ci == 0:
                        p.wait(t, f"lc_ge{f'cms{lv - 1}_{NCH - 1}'}_{HID // P - 2 + o}")
                    for k in range(KD):
                        ins = t.matmul(
                            ps[:, 0:TC], w1[:, k * HID + o * P:k * HID + (o + 1) * P],
                            h16[:, k * T + ci * TC:k * T + (ci + 1) * TC],
                            start=(k == 0), stop=(k == KD - 1))
                    p.mark(f"lc_up{cid}_{o}", ins, "t")

                @p.sec("s")
                def _(s, cid=cid, o=o, lv=lv, gcb=gcb):
                    p.wait(s, f"lc_up{cid}_{o}")
                    if o == 0:
                        gcb.wait_all(s)
                    ps = [p.ps_a, p.ps_b][o % 2]
                    p.mark(f"lc_ge{cid}_{o}", s.activation(
                        out=hid16[:, o * TC:(o + 1) * TC], in_=ps[:, 0:TC],
                        func=AF.Gelu_apprx_tanh, bias=b1v[:, o:o + 1]), "s")

            @p.sec("t")
            def _(t, lv=lv, ci=ci, cid=cid):
                p.wait(t, f"lc_ge{cid}_{HID // P - 1}")
                if ci >= 1:
                    p.wait(t, f"lc_dn{f'cms{lv}_{ci - 1}'}")
                for o in range(KD):
                    for k in range(HID // P):
                        ins = t.matmul(p.ps_y[:, o * TC:(o + 1) * TC],
                                       w2[:, k * D + o * P:k * D + (o + 1) * P],
                                       hid16[:, k * TC:(k + 1) * TC],
                                       start=(k == 0), stop=(k == HID // P - 1))
                p.mark(f"lc_dnmm{cid}", ins, "t")

            @p.sec("s")
            def _(s, lv=lv, ci=ci, cid=cid):
                p.wait(s, f"lc_dnmm{cid}")
                for o in range(KD):
                    s.activation(out=h16[:, o * T + ci * TC:o * T + (ci + 1) * TC],
                                 in_=p.ps_y[:, o * TC:(o + 1) * TC],
                                 func=AF.Identity, bias=b2v[:, o:o + 1])
                p.mark(f"lc_dn{cid}", s.drain(), "s")

        @p.sec("t")
        def _(t, lv=lv):
            p.mark(f"lc_lvl{lv}", t.drain(), "t")



def emit_l1d(p):
    h16 = p.h16_sb
    x1ic = p.sb("ld_x1i", [P, KD * TC], F32)
    gl = p.sb("ld_gl", [P, KD * TC], BF16)
    x1t = p.sb("ld_x1t", [P, 2 * D], F32)
    for ci in range(NCH):
        cid = f"fin{ci}"

        @p.sec("q")
        def _(q, ci=ci):
            st1 = p.stream("lc_fx1i", 2)
            st2 = p.stream("lc_fg1", 2)
            st3 = p.stream("lc_fx1", 2)
            if ci >= 1:
                p.wait(q, f"lc_x2{ci - 1}")
            st1.issue(q, x1ic.ap().rearrange("pp (o t) -> pp o t", o=KD),
                      p.x1i_h.rearrange("pp (o t) -> pp o t", t=T)[
                          :, :, ci * TC:(ci + 1) * TC])
            st2.issue(q, gl.ap().rearrange("pp (o t) -> pp o t", o=KD),
                      p.g1_h.rearrange("pp (o t) -> pp o t", t=T)[
                          :, :, ci * TC:(ci + 1) * TC])
            st3.issue(q, x1t.ap().rearrange("pp (a dd) -> pp a dd", a=2),
                      p.x1_h[ci * TC:(ci + 1) * TC, :].rearrange(
                          "(a pp) dd -> pp a dd", pp=P))

        @p.sec("v")
        def _(v, ci=ci, cid=cid):
            p.stream("lc_fx1i", 2).wait(v, ci)
            p.stream("lc_fg1", 2).wait(v, ci)
            if ci == 0:
                p.wait(v, f"lc_dn{f'cms{LEV - 1}_{NCH - 1}'}")
            for o in range(KD):
                sl = slice(o * TC, (o + 1) * TC)
                v.tensor_tensor(out=x1ic[:, sl], in0=x1ic[:, sl],
                                in1=h16[:, o * T + ci * TC:o * T + (ci + 1) * TC],
                                op=ALU.add)
            v.drain()
            v.tensor_tensor(out=x1ic[:], in0=x1ic[:], in1=gl[:], op=ALU.mult)
            p.mark(f"lc_sg{cid}", v.drain(), "v")

        @p.sec("t")
        def _(t, ci=ci, cid=cid):
            p.wait(t, f"lc_sg{cid}")
            if ci >= 1:
                p.wait(t, f"lc_x2v{ci - 1}")
            for h in range(2):
                for o in range(KD):
                    idx = h * KD + o
                    ins = t.transpose(p.ps_y[:, idx * P:(idx + 1) * P],
                                      x1ic[:, o * TC + h * P:o * TC + (h + 1) * P],
                                      p.ident_sb[:])
                    if idx % 8 == 7:
                        p.mark(f"lc_ftr{cid}_{idx // 8}", ins, "t")

        @p.sec("v")
        def _(v, ci=ci, cid=cid):
            p.stream("lc_fx1", 2).wait(v, ci)
            for h in range(2):
                p.wait(v, f"lc_ftr{cid}_{h}")
            for h in range(2):
                for o in range(KD):
                    idx = h * KD + o
                    v.tensor_tensor(out=x1t[:, h * D + o * P:h * D + (o + 1) * P],
                                    in0=p.ps_y[:, idx * P:(idx + 1) * P],
                                    in1=x1t[:, h * D + o * P:h * D + (o + 1) * P],
                                    op=ALU.add)
            p.mark(f"lc_x2v{ci}", v.drain(), "v")

        @p.sec("q")
        def _(q, ci=ci):
            p.wait(q, f"lc_x2v{ci}")
            st_ = p.stream("lc_out", 2)
            st_.issue(q, p.out_h[ci * TC:(ci + 1) * TC, :].rearrange(
                "(a pp) dd -> pp a dd", pp=P),
                x1t.ap().rearrange("pp (a dd) -> pp a dd", a=2))
            p.cnt["q"] += 1
            q.sem_inc(p.sem["q"], 1)
            p.ev.setdefault(f"lc_x2{ci}", []).append(("q", p.cnt["q"]))




def _prep_weights(inputs):
    f = np.ascontiguousarray

    def T_(a):
        return f(np.transpose(a).astype(np.float32))

    d = {}
    d["w_inT"] = T_(inputs["L0_attn_W_in"])
    cw_ = np.stack([inputs["L0_attn_conv_w"][:, 0, :], inputs["L0_attn_conv_w_b"][:, 0, :]]).astype(np.float32)
    d["conv_w"] = f(np.transpose(cw_.reshape(2, KDI, P, 4), (0, 2, 1, 3)).reshape(2, P, KDI * 4))
    cb_ = np.stack([inputs["L0_attn_conv_b"], inputs["L0_attn_conv_b_b"]]).astype(np.float32)
    d["conv_b"] = f(np.transpose(cb_.reshape(2, KDI, P), (0, 2, 1)))
    wx_ = np.stack([T_(inputs["L0_attn_Wx"]), T_(inputs["L0_attn_Wx_b"])])
    wxp = np.zeros((2, DI, 112), np.float32)
    wxp[:, :, 0:80] = wx_[:, :, 0:80]
    wxp[:, :, 96:112] = wx_[:, :, 80:96]
    d["wxT"] = f(wxp)
    d["wdtT"] = f(np.stack([T_(inputs["L0_attn_Wdt"]), T_(inputs["L0_attn_Wdt_b"])]))
    bd_ = np.stack([inputs["L0_attn_bdt"], inputs["L0_attn_bdt_b"]]).astype(np.float32)
    d["bdt"] = f(np.transpose(bd_.reshape(2, KDI, P), (0, 2, 1)))
    al = []
    for a in (inputs["L0_attn_A_log"], inputs["L0_attn_A_b_log"]):
        al.append(np.transpose(np.asarray(a, np.float32).reshape(KDI, P, NST),
                               (1, 0, 2)).reshape(P, KDI * NST))
    d["a_log"] = f(np.stack(al))
    ds_ = np.stack([inputs["L0_attn_D"], inputs["L0_attn_D_b"]]).astype(np.float32)
    d["d_skip"] = f(np.transpose(ds_.reshape(2, KDI, P), (0, 2, 1)))
    d["w_outT"] = T_(inputs["L0_attn_W_out"])
    d["wr_T"] = T_(inputs["L0_ffn_Wr"])
    d["br"] = f(np.asarray(inputs["L0_ffn_br"], np.float32)[None, :])
    d["biases_n"] = f(np.asarray(inputs["L0_ffn_biases_N"], np.float32)[None, :])
    d["wl1"] = f(np.asarray(inputs["L0_ffn_Wl1"], np.float32))
    d["bl1"] = f(np.transpose(np.asarray(inputs["L0_ffn_bl1"], np.float32).reshape(NEXP, 32, P), (0, 2, 1)))
    d["wl2"] = f(np.asarray(inputs["L0_ffn_Wl2"], np.float32))
    d["bl2"] = f(np.transpose(np.asarray(inputs["L0_ffn_bl2"], np.float32).reshape(NEXP, 8, P), (0, 2, 1)))
    d["bl2r"] = f(np.asarray(inputs["L0_ffn_bl2"], np.float32))
    d["ln1_g"] = f(np.asarray(inputs["L1_attn_ln1_g"], np.float32).reshape(KD, P).T.copy())
    d["ln1_b"] = f(np.asarray(inputs["L1_attn_ln1_b"], np.float32).reshape(KD, P).T.copy())
    d["wq"] = f(np.asarray(inputs["L1_attn_wq"], np.float32))
    d["wk"] = f(np.asarray(inputs["L1_attn_wk"], np.float32))
    d["wv"] = f(np.asarray(inputs["L1_attn_wv"], np.float32))
    wo = np.asarray(inputs["L1_attn_wo"], np.float32).reshape(HEADS, HD, D)
    kt = np.empty((KD, P, D), np.float32)
    for hp in range(KD):
        kt[hp, 0:HD] = wo[2 * hp]
        kt[hp, HD:P] = wo[2 * hp + 1]
    d["wo_kt"] = f(kt.reshape(D, D))
    d["ln2_g"] = f(np.asarray(inputs["L1_attn_ln2_g"], np.float32).reshape(KD, P).T.copy())
    d["ln2_b"] = f(np.asarray(inputs["L1_attn_ln2_b"], np.float32).reshape(KD, P).T.copy())
    d["cms_w1"] = f(np.asarray(inputs["L1_attn_cms_w1"], np.float32))
    d["cms_b1"] = f(np.transpose(np.asarray(inputs["L1_attn_cms_b1"], np.float32).reshape(LEV, HID // P, P), (0, 2, 1)))
    d["cms_w2"] = f(np.asarray(inputs["L1_attn_cms_w2"], np.float32))
    d["cms_b2"] = f(np.transpose(np.asarray(inputs["L1_attn_cms_b2"], np.float32).reshape(LEV, KD, P), (0, 2, 1)))
    d["ffn_wT"] = T_(inputs["L1_ffn_W"])
    d["ffn_b"] = f(np.asarray(inputs["L1_ffn_b"], np.float32).reshape(KD, P).T.copy())
    d["ffn_vr"] = f(np.asarray(inputs["L1_ffn_V_r"], np.float32))
    d["ffn_cT"] = T_(inputs["L1_ffn_C"])
    return d


def run(inputs, dbg_spec=(), upto="full", cores=8, trace=False):
    w = _prep_weights(inputs)
    hidden = np.asarray(inputs["L0_attn_hidden"], np.float32)
    prog = build(dbg_spec, upto=upto)
    in_maps = []
    for b in range(cores):
        m = dict(w)
        m["x0"] = np.ascontiguousarray(hidden[b])
        in_maps.append(m)
    res = run_bass_kernel_spmd(prog.nc, in_maps, core_ids=list(range(cores)),
                               trace=trace)
    return res


def kernel(**inputs):
    res = run(inputs, upto="full")
    out = np.stack([np.asarray(r["out"]) for r in res.results])
    return out.astype(np.float32)
